# version 10
# baseline (speedup 1.0000x reference)
import numpy as np
from contextlib import ExitStack

import concourse.bass as bass
import concourse.mybir as mybir
from concourse.bass_utils import run_bass_kernel_spmd

F32 = mybir.dt.float32
BF16 = mybir.dt.bfloat16
AF = mybir.ActivationFunctionType
ALU = mybir.AluOpType
AX = mybir.AxisListType

D = 2048
NH = 16
RK = 32
HF = 8192
TOK = 2048
T = 8192
NTB = TOK // 128
NDB = D // 128
NKT = T // 128
SCALE = 1.0 / float(np.sqrt(RK))
LN_EPS = 1e-5
RMS_EPS = float(np.finfo(np.float32).eps)
SUB = 512
NSUB = TOK // SUB


class Rec:
    ENGS = ("tensor", "vector", "scalar", "gpsimd", "sync")

    def __init__(self):
        self.q = {e: [] for e in self.ENGS}

    def emit(self, nc, name):
        with nc.Block(name) as blk:
            for e in self.ENGS:
                fns = self.q[e]
                if fns:
                    getattr(blk, e)(lambda eng, fns=fns: [f(eng) for f in fns])
        self.q = {e: [] for e in self.ENGS}


def build_nc(limit=None, debug=False):
    nc = bass.Bass("TRN2", target_bir_lowering=False)

    x_in = nc.dram_tensor("x", [TOK, D], F32, kind="ExternalInput")
    wq_in = nc.dram_tensor("Wq", [D, D], F32, kind="ExternalInput")
    wk_in = nc.dram_tensor("Wk", [D, D], F32, kind="ExternalInput")
    wv_in = nc.dram_tensor("Wv", [D, D], F32, kind="ExternalInput")
    wo_in = nc.dram_tensor("Wo", [D, D], F32, kind="ExternalInput")
    wdown_in = nc.dram_tensor("Wdown", [128, RK], F32, kind="ExternalInput")
    wup_in = nc.dram_tensor("Wup", [RK, 128], F32, kind="ExternalInput")
    wgu_sl = nc.dram_tensor("wgu_sl", [2 * HF, 256], F32, kind="ExternalInput")
    wpj_sl = nc.dram_tensor("wpj_sl", [D, 1024], F32, kind="ExternalInput")
    y_out = nc.dram_tensor("y", [TOK, D], F32, kind="ExternalOutput")

    ik = "ExternalOutput" if debug else "Internal"
    x2_dram = nc.dram_tensor("x2_dram", [TOK, D], F32, kind=ik)
    kc_dram = nc.dram_tensor("kc_dram", [512, TOK], BF16)
    kg_dram = nc.dram_tensor("kg_dram", [2048, TOK], BF16)
    vc_dram = nc.dram_tensor("vc_dram", [TOK, 512], BF16)
    vg_dram = nc.dram_tensor("vg_dram", [T, 512], BF16)
    kmax_dram = nc.dram_tensor("kmax_dram", [512], F32)
    kmaxg_dram = nc.dram_tensor("kmaxg_dram", [4 * 512], F32)
    wguc_dram = nc.dram_tensor("wguc_dram", [32, 128, 2, 512], BF16)
    wgug_dram = nc.dram_tensor("wgug_dram", [8, 32, 128, 2, 512], BF16)
    wpjc_dram = nc.dram_tensor("wpjc_dram", [1024, D], BF16)
    wpjg_dram = nc.dram_tensor("wpjg_dram", [HF, D], BF16)
    ln3T_dram = nc.dram_tensor("ln3T_dram", [NDB, 128, TOK], BF16, kind=ik)
    if debug:
        dbg_kg = nc.dram_tensor("dbg_kg", [2048, TOK], BF16, kind="ExternalOutput")
        dbg_vg = nc.dram_tensor("dbg_vg", [T, 512], BF16, kind="ExternalOutput")
        dbg_kc = nc.dram_tensor("dbg_kc", [512, TOK], BF16, kind="ExternalOutput")
        dbg_vc = nc.dram_tensor("dbg_vc", [TOK, 512], BF16, kind="ExternalOutput")
        dbg_q = nc.dram_tensor("dbg_q", [128, 4, TOK], BF16, kind="ExternalOutput")
        dbg_y = nc.dram_tensor("dbg_y", [128, 4, TOK], BF16, kind="ExternalOutput")
        dbg_fold = nc.dram_tensor("dbg_fold", [128, 4, D], BF16, kind="ExternalOutput")
        dbg_wqr = nc.dram_tensor("dbg_wqr", [128, NDB, 512], BF16, kind="ExternalOutput")
        dbg_bias = nc.dram_tensor("dbg_bias", [128, 8], F32, kind="ExternalOutput")
    ident_dram = nc.inline_tensor(np.eye(128, dtype=np.float32), name="ident_f32")

    es0 = ExitStack()

    def sb(stk, name, shape, dtype):
        return stk.enter_context(nc.sbuf_tensor(name, shape, dtype))

    def pst(stk, name, shape, dtype=F32):
        return stk.enter_context(nc.psum_tensor(name, shape, dtype))

    class Ctr:
        def __init__(self, name):
            self.sem = es0.enter_context(nc.semaphore(name))
            self.n = 0

    r = Rec()

    def on(eng, fn):
        r.q[eng].append(fn)

    def inc(eng, mk, ctr, k=1):
        ctr.n += k
        r.q[eng].append(lambda e, mk=mk, s=ctr.sem, k=k: mk(e).then_inc(s, k))

    def bump(eng, ctr, k=1):
        ctr.n += k
        last = r.q[eng].pop()
        r.q[eng].append(lambda e, last=last, sm=ctr.sem, k=k: last(e).then_inc(sm, k))

    def wait(eng, ctr, n=None):
        n0 = ctr.n if n is None else n
        if n0 > 0:
            r.q[eng].append(lambda e, s=ctr.sem, n=n0: e.wait_ge(s, n))

    identb = sb(es0, "identb", [128, 128], BF16)
    ones_col = sb(es0, "ones_col", [128, 1], BF16)
    ones_colF = sb(es0, "ones_colF", [128, 1], F32)
    ones_r32 = sb(es0, "ones_r32", [1, 32], F32)
    ones_r128 = sb(es0, "ones_r128", [1, 128], F32)
    biasp = sb(es0, "biasp", [128, 8], F32)
    wdownq = sb(es0, "wdownq", [128, RK], BF16)
    wdownk = sb(es0, "wdownk", [128, RK], BF16)
    wupT = sb(es0, "wupT", [128, RK], BF16)
    nrm = sb(es0, "nrm", [128, 16], F32)
    brow = sb(es0, "brow", [1, 32], F32)
    inv_sb = sb(es0, "inv_sb", [128, 512], F32)
    stat = sb(es0, "stat", [128, 32], F32)
    inv_t = sb(es0, "inv_t", [128, 4], F32)
    wup_sb0 = sb(es0, "wup_sb0", [RK, 128], BF16)
    kmax_acc = sb(es0, "kmax_acc", [128, 8], F32)
    kmxt = sb(es0, "kmxt", [128, 16], F32)

    sLD = Ctr("sLD")
    sCONST = Ctr("sCONST")
    sWT = [Ctr("sWT0"), Ctr("sWT1")]
    sWO = [Ctr("sWO0"), Ctr("sWO1")]
    sTS = [Ctr("sTS0"), Ctr("sTS1")]
    sKV = Ctr("sKV")
    sKVL = Ctr("sKVL")
    sX2S = Ctr("sX2S")
    sL3S = Ctr("sL3S")
    sL3 = Ctr("sL3")
    sX2 = Ctr("sX2")
    sDBG = Ctr("sDBG")
    sOUT2 = [Ctr("sOUT0"), Ctr("sOUT1")]
    sFWs = [Ctr("sFWa"), Ctr("sFWb")]
    sPJ = [Ctr("sPJa"), Ctr("sPJb")]
    sKC = [Ctr("sKC0"), Ctr("sKC1")]
    sKC2 = [Ctr("sKC2a"), Ctr("sKC2b")]
    sTR = [Ctr("sTR0"), Ctr("sTR1")]
    sKM = Ctr("sKM")
    sKM2 = Ctr("sKM2")
    sGD = Ctr("sGD")
    sTE = Ctr("sTE")
    sVE = Ctr("sVE")
    sSE = Ctr("sSE")
    sCC = Ctr("sCC")
    sOUT = Ctr("sOUT")
    sSC = Ctr("sSC")
    sEX = Ctr("sEX")
    sYM = Ctr("sYM")
    sQ2 = Ctr("sQ2")
    sXL = Ctr("sXL")
    sFW = Ctr("sFW")

    es1 = ExitStack()
    q_rT = sb(es1, "q_rT", [128, 4, TOK], BF16)
    fold = sb(es1, "fold", [128, 4, D], BF16)
    y_rT = sb(es1, "y_rT", [128, 4, TOK], BF16)

    def ln_stats(tiles, scol):
        nt = len(tiles)
        cols = []
        wait("vector", sVE)
        for i, ap_fn in enumerate(tiles):
            c0 = scol + 4 * i
            on("vector", lambda v, f=ap_fn, c0=c0: v.reduce_sum(stat[:, c0:c0 + 1], f(), AX.X))
            on("vector", lambda v, f=ap_fn, c0=c0: v.scalar_tensor_tensor(
                lnscr[:, :], f(), 1.0, f(), ALU.mult, ALU.mult,
                accum_out=stat[:, c0 + 1:c0 + 2]))
            bump("vector", sVE)
            wait("vector", sVE)
            cols.append((c0 + 2, c0 + 3))
        for i in range(nt):
            c0 = scol + 4 * i
            on("vector", lambda v, c0=c0: v.tensor_scalar(
                stat[:, c0 + 2:c0 + 3], stat[:, c0:c0 + 1], -1.0 / D, None, ALU.mult))
        bump("vector", sVE)
        wait("vector", sVE)
        for i in range(nt):
            c0 = scol + 4 * i
            on("vector", lambda v, c0=c0: v.tensor_tensor(
                stat[:, c0:c0 + 1], stat[:, c0 + 2:c0 + 3], stat[:, c0 + 2:c0 + 3], ALU.mult))
        bump("vector", sVE)
        wait("vector", sVE)
        for i in range(nt):
            c0 = scol + 4 * i
            on("vector", lambda v, c0=c0: v.tensor_scalar(
                stat[:, c0 + 3:c0 + 4], stat[:, c0 + 1:c0 + 2], 1.0 / D, LN_EPS, ALU.mult, ALU.add))
        bump("vector", sVE)
        wait("vector", sVE)
        for i in range(nt):
            c0 = scol + 4 * i
            on("vector", lambda v, c0=c0: v.tensor_tensor(
                stat[:, c0 + 3:c0 + 4], stat[:, c0 + 3:c0 + 4], stat[:, c0:c0 + 1], ALU.subtract))
        bump("vector", sVE)
        ve_stats = sVE.n
        wait("scalar", sVE, ve_stats)
        for i in range(nt):
            c0 = scol + 4 * i
            fn = lambda s, c0=c0: s.activation(
                out=stat[:, c0 + 3:c0 + 4], in_=stat[:, c0 + 3:c0 + 4], func=AF.Ln)
            if i == nt - 1:
                inc("scalar", fn, sSE)
            else:
                on("scalar", fn)
        wait("scalar", sSE)
        for i in range(nt):
            c0 = scol + 4 * i
            fn = lambda s, c0=c0: s.activation(
                out=stat[:, c0 + 3:c0 + 4], in_=stat[:, c0 + 3:c0 + 4], func=AF.Exp, scale=-0.5)
            if i == nt - 1:
                inc("scalar", fn, sSE)
            else:
                on("scalar", fn)
        return cols

    on("gpsimd", lambda g: g.memset(ones_col[:, :], 1.0))
    on("gpsimd", lambda g: g.memset(ones_colF[:, :], 1.0))
    on("gpsimd", lambda g: g.memset(ones_r32[:, :], 1.0))
    on("gpsimd", lambda g: g.memset(ones_r128[:, :], 1.0))
    inc("gpsimd", lambda g: g.dma_start(out=identb[:, :], in_=ident_dram.ap()[:, :]), sCONST, 16)
    inc("gpsimd", lambda g: g.dma_start(out=wdownk[:, :], in_=wdown_in.ap()[:, :]), sCONST, 16)
    inc("gpsimd", lambda g: g.dma_start(out=wup_sb0[:, :], in_=wup_in.ap()[:, :]), sCONST, 16)

    esFA = ExitStack()
    wqr = sb(esFA, "wqr", [128, NDB, 512], BF16)
    wkr = sb(esFA, "wkr", [128, NDB, 512], BF16)
    wvr = sb(esFA, "wvr", [128, NDB, 512], BF16)

    esf = ExitStack()
    wtile = sb(esf, "wtile", [128, 2, 3, D], BF16)
    wotile = sb(esf, "wotile", [128, 2, NDB, 128], BF16)
    wstrip = sb(esf, "wstrip", [128, 2, D], BF16)
    p_f = pst(esf, "p_f", [128, 3, 128], F32)
    p_w = pst(esf, "p_w", [128, 512], BF16)
    p_o = [pst(esf, "p_o%d" % i, [128, 512], F32) for i in range(4)]
    p_s = pst(esf, "p_s", [128, RK], BF16)

    wait("vector", sCONST, 48)
    inc("vector", lambda v: v.tensor_scalar(wdownq[:, :], wdownk[:, :], SCALE, None, ALU.mult), sVE)
    ve_wd = sVE.n
    wait("tensor", sCONST, 48)
    inc("tensor", lambda t: t.transpose(p_s[:, :], wup_sb0[:, :], identb[0:RK, 0:RK]), sTE)
    wait("vector", sTE)
    inc("vector", lambda v: v.tensor_copy(wupT[:, :], p_s[:, :]), sVE)
    ve_wu = sVE.n

    if limit == "f1":
        r.emit(nc, "phase_f")
        return nc
    wq_rows = wq_in.ap().rearrange("(n p) d -> p n d", p=128)
    wk_rows = wk_in.ap().rearrange("(n p) d -> p n d", p=128)
    wv_rows = wv_in.ap().rearrange("(n p) d -> p n d", p=128)
    x_rows = x_in.ap().rearrange("(n p) d -> p n d", p=128)
    mm_done = {}
    ev_done = {}
    for h in range(NH):
        bi = h % 2
        if h >= 2:
            wait("gpsimd", sTE, mm_done[h - 2])
        inc("gpsimd", lambda g, h=h, bi=bi: g.dma_start(out=wtile[:, bi, 0, :], in_=wq_rows[:, h, :]), sWT[bi], 16)
        inc("gpsimd", lambda g, h=h, bi=bi: g.dma_start(out=wtile[:, bi, 1, :], in_=wk_rows[:, h, :]), sWT[bi], 16)
        inc("gpsimd", lambda g, h=h, bi=bi: g.dma_start(out=wtile[:, bi, 2, :], in_=wv_rows[:, h, :]), sWT[bi], 16)
        ldh = sWT[bi].n
        if limit == "f2a":
            mm_done[h] = 0
            for dbg in range(4):
                ev_done[(h, dbg)] = 0
            continue
        wait("tensor", sWT[bi], ldh)
        wait("tensor", sVE, max(ve_wd, ve_wu))
        for dbg in range(4):
            wait("tensor", sVE, ev_done.get((h, dbg - 1), ev_done.get((h - 1, 3), 0)))
            for db in range(4 * dbg, 4 * dbg + 4):
                g4 = db % 4
                for w_i in range(3):
                    wd = wdownq if w_i == 0 else wdownk
                    fn = lambda t, bi=bi, db=db, g4=g4, w_i=w_i, wd=wd: \
                        t.matmul(p_f[:, w_i, 32 * g4:32 * (g4 + 1)],
                                 wtile[:, bi, w_i, 128 * db:128 * (db + 1)], wd[:, :],
                                 start=True, stop=True)
                    if db == 4 * dbg + 3 and w_i == 2:
                        inc("tensor", fn, sTE)
                    else:
                        on("tensor", fn)
            if limit == "f2b":
                ev_done[(h, dbg)] = 0
                continue
            wait("vector", sTE)
            for w_i, dstw in ((0, wqr), (1, wkr), (2, wvr)):
                for g4 in range(4):
                    on("vector", lambda v, dstw=dstw, w_i=w_i, dbg=dbg, h=h, g4=g4:
                       v.tensor_copy(dstw.ap()[:, 4 * dbg + g4, 32 * h:32 * (h + 1)],
                                     p_f[:, w_i, 32 * g4:32 * (g4 + 1)]))
            bump("vector", sVE)
            ev_done[(h, dbg)] = sVE.n
        mm_done[h] = sTE.n

    if limit in ("f2", "f2a", "f2b", "skipq_f2", "f2d"):
        if debug and limit in ("skipq_f2", "f2d"):
            wait("sync", sVE)
            inc("sync", lambda sy: sy.dma_start(out=dbg_wqr.ap()[:, :, :], in_=wqr[:, :, :]), sDBG, 16)
            wait("sync", sDBG)
        r.emit(nc, "phase_f")
        return nc
    strip_tposed = {}
    foldmm = {}
    foldev = {}
    for c in range(4):
        for m in range(4):
            h = 4 * c + m
            bi = h % 2
            if h >= 2:
                wait("gpsimd", sSC, strip_tposed[h - 2])
            inc("gpsimd", lambda g, h=h, bi=bi: g.dma_start(
                out=wotile[:, bi, :, :],
                in_=wo_in.ap()[:, 128 * h:128 * (h + 1)].rearrange("(n p) d -> p n d", p=128)), sWO[bi], 16)
            ldo = sWO[bi].n
            wait("tensor", sWO[bi], ldo)
            for dbg in range(4):
                wait("tensor", sVE)
                for db in range(4 * dbg, 4 * dbg + 4):
                    fn = lambda t, bi=bi, db=db: \
                        t.transpose(p_w[:, 128 * (db % 4):128 * (db % 4 + 1)],
                                    wotile[:, bi, db, :], identb[:, :])
                    if db == 4 * dbg + 3:
                        inc("tensor", fn, sSC)
                    else:
                        on("tensor", fn)
                wait("vector", sSC)
                if h >= 2:
                    wait("vector", sTE, foldmm[h - 2])
                on("vector", lambda v, bi=bi, dbg=dbg:
                   v.tensor_copy(wstrip[:, bi, 512 * dbg:512 * (dbg + 1)], p_w[:, :]))
                bump("vector", sVE)
            strip_tposed[h] = sSC.n
            wait("tensor", sVE)
            if m == 0 and c > 0:
                wait("tensor", sVE, foldev[c - 1])
            for dc in range(4):
                fn = lambda t, m=m, bi=bi, dc=dc: \
                    t.matmul(p_o[dc][32 * m:32 * (m + 1), :], wupT[:, :],
                             wstrip[:, bi, 512 * dc:512 * (dc + 1)],
                             start=True, stop=True, tile_position=(0, 32 * m))
                if dc == 3:
                    inc("tensor", fn, sTE)
                else:
                    on("tensor", fn)
            foldmm[h] = sTE.n
            if m == 3:
                wait("vector", sTE, foldmm[h])
                for dc in range(4):
                    on("vector", lambda v, c=c, dc=dc:
                       v.tensor_copy(fold[:, c, 512 * dc:512 * (dc + 1)], p_o[dc][:, :]))
                bump("vector", sVE)
                foldev[c] = sVE.n

    r.emit(nc, "phase_f")
    if limit == "f":
        return nc
    esf.close()

    esa = ExitStack()
    lnscr = sb(esa, "lnscr_a", [128, D], F32)
    xln = sb(esa, "xln", [128, 4, D], F32)
    ln1b = sb(esa, "ln1b", [128, 4, D], BF16)
    ln1T = sb(esa, "ln1T", [128, NDB, 512], BF16)
    kcon = sb(esa, "kcon", [128, 4, 512], BF16)
    vcon = sb(esa, "vcon", [128, 4, 512], BF16)
    p_ln = pst(esa, "p_ln", [128, 2, 1024], BF16)
    p_qk = pst(esa, "p_qk", [128, 2, 512], F32)
    p_v = pst(esa, "p_v", [128, 2, 512], F32)

    if debug:
        wait("sync", sVE)
        inc("sync", lambda sy: sy.dma_start(out=dbg_wqr.ap()[:, :, :], in_=wqr[:, :, :]), sDBG, 16)
    apply_mark = 0
    tpose_mark = 0
    qkv_mark = 0
    kv_dma_mark = 0
    evq = {}
    evv = {}
    for tc in range(4):
        wait("sync", sVE, apply_mark)
        for tb4 in range(4):
            ti = 4 * tc + tb4
            inc("sync", lambda s, ti=ti, tb4=tb4: s.dma_start(out=xln[:, tb4, :], in_=x_rows[:, ti, :]), sXL, 16)
        ldx = sXL.n
        wait("vector", sXL, ldx)
        wait("vector", sSC, tpose_mark)
        cols = ln_stats([(lambda tb4=tb4: xln.ap()[:, tb4, :]) for tb4 in range(4)], 0)
        wait("vector", sSE)
        for tb4 in range(4):
            nm, iv = cols[tb4]
            on("vector", lambda v, tb4=tb4, nm=nm, iv=iv:
               v.tensor_scalar(ln1b[:, tb4, :], xln[:, tb4, :],
                               stat[:, nm:nm + 1], stat[:, iv:iv + 1], ALU.add, ALU.mult))
        bump("vector", sVE)
        apply_mark = sVE.n
        wait("tensor", sVE, apply_mark)
        wait("tensor", sSC, qkv_mark)
        for db in range(NDB):
            pbi = db % 2
            if db >= 2:
                wait("tensor", sVE, evq.get(("lt", tc, db - 2), 0))
            for tb4 in range(4):
                fn = lambda t, tb4=tb4, db=db, pbi=pbi: \
                    t.transpose(p_ln[:, pbi, 128 * tb4:128 * (tb4 + 1)],
                                ln1b[:, tb4, 128 * db:128 * (db + 1)], identb[:, :])
                if tb4 == 3:
                    inc("tensor", fn, sSC)
                else:
                    on("tensor", fn)
            wait("vector", sSC)
            on("vector", lambda v, db=db, pbi=pbi: v.tensor_copy(ln1T[:, db, :], p_ln[:, pbi, 0:512]))
            bump("vector", sVE)
            evq[("lt", tc, db)] = sVE.n
        tpose_mark = sSC.n
        lt_done = sVE.n
        wait("tensor", sVE, lt_done)
        for qk in range(2):
            wmat = wqr if qk == 0 else wkr
            for cc in range(4):
                pbi = cc % 2
                wait("tensor", sVE, evq.get(("qk", tc, qk * 4 + cc - 2), 0))
                for db in range(NDB):
                    fn = lambda t, wmat=wmat, cc=cc, db=db, pbi=pbi: \
                        t.matmul(p_qk[:, pbi, :], wmat[:, db, 128 * cc:128 * (cc + 1)],
                                 ln1T[:, db, :], start=(db == 0), stop=(db == NDB - 1))
                    if db == NDB - 1:
                        inc("tensor", fn, sSC)
                    else:
                        on("tensor", fn)
                wait("vector", sSC)
                if qk == 1:
                    wait("vector", sKV, kv_dma_mark)
                if qk == 0:
                    on("vector", lambda v, cc=cc, tc=tc, pbi=pbi:
                       v.tensor_copy(q_rT[:, cc, 512 * tc:512 * (tc + 1)], p_qk[:, pbi, :]))
                else:
                    on("vector", lambda v, cc=cc, pbi=pbi:
                       v.tensor_copy(kcon[:, cc, :], p_qk[:, pbi, :]))
                bump("vector", sVE)
                evq[("qk", tc, qk * 4 + cc)] = sVE.n
        wait("vector", sVE)
        for cc in range(4):
            dstc = cc if tc == 0 else 4 + cc
            on("vector", lambda v, cc=cc, dstc=dstc: v.reduce_max(
                kmax_acc[:, dstc:dstc + 1], kcon[:, cc, :], AX.X, apply_absolute_value=True))
        bump("vector", sVE)
        if tc > 0:
            wait("vector", sVE)
            for cc in range(4):
                on("vector", lambda v, cc=cc: v.tensor_tensor(
                    kmax_acc[:, cc:cc + 1], kmax_acc[:, cc:cc + 1],
                    kmax_acc[:, 4 + cc:5 + cc], ALU.max))
            bump("vector", sVE)
        for tb4 in range(4):
            pbi = tb4 % 2
            wait("tensor", sVE, evv.get((tc, tb4 - 2), 0))
            for db in range(NDB):
                fn = lambda t, tb4=tb4, db=db, pbi=pbi: \
                    t.matmul(p_v[:, pbi, :], ln1T[:, db, 128 * tb4:128 * (tb4 + 1)],
                             wvr[:, db, :], start=(db == 0), stop=(db == NDB - 1))
                if db == NDB - 1:
                    inc("tensor", fn, sSC)
                else:
                    on("tensor", fn)
            wait("vector", sSC)
            wait("vector", sKV, kv_dma_mark)
            on("vector", lambda v, tb4=tb4, pbi=pbi: v.tensor_copy(vcon[:, tb4, :], p_v[:, pbi, :]))
            bump("vector", sVE)
            evv[(tc, tb4)] = sVE.n
        qkv_mark = sSC.n
        wait("sync", sVE)
        for cc in range(4):
            inc("sync", lambda s, cc=cc, tc=tc: s.dma_start(
                out=kc_dram.ap()[128 * cc:128 * (cc + 1), 512 * tc:512 * (tc + 1)],
                in_=kcon[:, cc, :]), sKV, 16)
        for tb4 in range(4):
            inc("sync", lambda s, tb4=tb4, tc=tc: s.dma_start(
                out=vc_dram.ap()[512 * tc + 128 * tb4:512 * tc + 128 * (tb4 + 1), :],
                in_=vcon[:, tb4, :]), sKV, 16)
        kv_dma_mark = sKV.n

    wait("sync", sVE)
    inc("sync", lambda s: s.dma_start(
        out=kmax_dram.ap().rearrange("(c p) -> p c", p=128), in_=kmax_acc[:, 0:4],
        allow_slow_non_contiguous=True), sKM, 16)
    wait("gpsimd", sKV)
    inc("gpsimd", lambda g: g.collective_compute(
        "AllGather", ALU.bypass, replica_groups=[[0, 1, 2, 3], [4, 5, 6, 7]],
        ins=[kc_dram.ap().opt()], outs=[kg_dram.ap().opt()]), sCC)
    inc("gpsimd", lambda g: g.collective_compute(
        "AllGather", ALU.bypass, replica_groups=[[0, 1, 2, 3], [4, 5, 6, 7]],
        ins=[vc_dram.ap().opt()], outs=[vg_dram.ap().opt()]), sCC)
    AGKV = sCC.n
    wait("gpsimd", sKM, 16)
    inc("gpsimd", lambda g: g.collective_compute(
        "AllGather", ALU.bypass, replica_groups=[[0, 1, 2, 3], [4, 5, 6, 7]],
        ins=[kmax_dram.ap().opt()], outs=[kmaxg_dram.ap().opt()]), sCC)
    KMX = sCC.n

    r.emit(nc, "phase_a")
    if limit == "a":
        return nc
    esa.close()
    esFA.close()

    esb = ExitStack()
    k_cc = sb(esb, "k_cc", [128, 2, 4, TOK], BF16)
    recrow = sb(esb, "recrow", [1, 1024], F32)
    v_full = sb(esb, "v_full", [128, NKT, 512], BF16)
    expT = sb(esb, "expT", [128, 4, 1024], BF16)

    kg_view = kg_dram.ap().rearrange("(r c p) t -> p c r t", p=128, c=4)
    vg_view = vg_dram.ap().rearrange("(n p) h -> p n h", p=128)
    wait("sync", sCC, AGKV - 1)
    inc("sync", lambda s: s.dma_start(out=k_cc[:, 0, :, :], in_=kg_view[:, 0, :, :]), sKC[0], 16)
    inc("sync", lambda s: s.dma_start(out=k_cc[:, 1, :, :], in_=kg_view[:, 1, :, :]), sKC[1], 16)
    wait("sync", sCC, AGKV)
    for part in range(4):
        inc("sync", lambda s, part=part: s.dma_start(
            out=v_full[:, 16 * part:16 * (part + 1), :],
            in_=vg_view[:, 16 * part:16 * (part + 1), :]), sKVL, 16)
    kv_ld = sKVL.n
    wait("sync", sCC, KMX)
    inc("sync", lambda s: s.dma_start(
        out=kmxt[:, :],
        in_=kmaxg_dram.ap().rearrange("(r c p) -> p (r c)", p=128, c=4),
        allow_slow_non_contiguous=True), sKM2, 16)

    esbias = ExitStack()
    p_b = [pst(esbias, "p_b%d" % i, [1, 4], F32) for i in range(4)]
    p_bb = pst(esbias, "p_bb", [128, 8], F32)

    wait("vector", sKM2, 16)
    for cc in range(4):
        on("vector", lambda v, cc=cc: v.reduce_max(
            nrm[:, cc:cc + 1], q_rT.ap()[:, cc, :], AX.X, apply_absolute_value=True))
    on("vector", lambda v: v.tensor_tensor(
        nrm[:, 4:8], kmxt[:, 0:4], kmxt[:, 4:8], ALU.max))
    bump("vector", sVE)
    wait("vector", sVE)
    on("vector", lambda v: v.tensor_tensor(
        nrm[:, 4:8], nrm[:, 4:8], kmxt[:, 8:12], ALU.max))
    bump("vector", sVE)
    wait("vector", sVE)
    on("vector", lambda v: v.tensor_tensor(
        nrm[:, 4:8], nrm[:, 4:8], kmxt[:, 12:16], ALU.max))
    bump("vector", sVE)
    wait("vector", sVE)
    for cc in range(4):
        on("vector", lambda v, cc=cc: v.tensor_tensor(
            nrm[:, 8 + cc:9 + cc], nrm[:, cc:cc + 1], nrm[:, 4 + cc:5 + cc], ALU.mult))
    bump("vector", sVE)
    wait("tensor", sVE)
    for s4 in range(4):
        for cc in range(4):
            fn = lambda t, s4=s4, cc=cc: \
                t.matmul(p_b[s4][0:1, cc:cc + 1], ones_colF[32 * s4:32 * (s4 + 1), :],
                         nrm[32 * s4:32 * (s4 + 1), 8 + cc:9 + cc],
                         start=True, stop=True, tile_position=(32 * s4, 0))
            if s4 == 3 and cc == 3:
                inc("tensor", fn, sTE)
            else:
                on("tensor", fn)
    wait("vector", sTE)
    for s4 in range(4):
        on("vector", lambda v, s4=s4: v.tensor_copy(brow[0:1, 4 * s4:4 * s4 + 4], p_b[s4][0:1, :]))
    bump("vector", sVE)
    wait("vector", sVE)
    for cc in range(4):
        for m in range(2):
            pp = 2 * cc + m
            on("vector", lambda v, cc=cc, m=m, pp=pp: v.tensor_tensor(
                brow[0:1, 16 + pp:17 + pp], brow[0:1, 4 * (2 * m) + cc:4 * (2 * m) + cc + 1],
                brow[0:1, 4 * (2 * m + 1) + cc:4 * (2 * m + 1) + cc + 1], ALU.max))
    bump("vector", sVE)
    wait("vector", sVE)
    inc("vector", lambda v: v.tensor_scalar(brow[0:1, 16:24], brow[0:1, 16:24], -1.0, None, ALU.mult), sVE)
    wait("tensor", sVE)
    inc("tensor", lambda t: t.matmul(p_bb[:, :], ones_r128[:, :], brow[0:1, 16:24],
                                     start=True, stop=True), sTE)
    wait("vector", sTE)
    inc("vector", lambda v: v.tensor_copy(biasp[:, :], p_bb[:, :]), sVE)
    bias_ready = sVE.n

    if debug:
        wait("sync", sVE, bias_ready)
        inc("sync", lambda sy: sy.dma_start(out=dbg_kg.ap()[:, :], in_=kg_dram.ap()[:, :]), sDBG, 16)
        inc("sync", lambda sy: sy.dma_start(out=dbg_vg.ap()[:, :], in_=vg_dram.ap()[:, :]), sDBG, 16)
        inc("sync", lambda sy: sy.dma_start(out=dbg_kc.ap()[:, :], in_=kc_dram.ap()[:, :]), sDBG, 16)
        inc("sync", lambda sy: sy.dma_start(out=dbg_vc.ap()[:, :], in_=vc_dram.ap()[:, :]), sDBG, 16)
        inc("sync", lambda sy: sy.dma_start(out=dbg_q.ap()[:, :, :], in_=q_rT[:, :, :]), sDBG, 16)
        inc("sync", lambda sy: sy.dma_start(out=dbg_fold.ap()[:, :, :], in_=fold[:, :, :]), sDBG, 16)
        inc("sync", lambda sy: sy.dma_start(out=dbg_bias.ap()[:, :], in_=biasp[:, :]), sDBG, 16)
    esbias.close()
    p_sc = pst(esb, "p_sc", [128, 2, 1024], F32)
    p_yA = pst(esb, "p_yA", [128, 512], F32)
    p_yB = pst(esb, "p_yB", [128, 512], F32)
    p_da = pst(esb, "p_da", [128, 512], F32)
    p_db = pst(esb, "p_db", [128, 512], F32)

    esq = ExitStack()
    q_bf = sb(esq, "q_bf", [128, 2048], BF16)
    q_aw = sb(esq, "q_aw", [128, 2048], BF16)
    q_sg = sb(esq, "q_sg", [128, 2048], BF16)
    q_ht = sb(esq, "q_ht", [128, 2048], BF16)
    q_gs = sb(esq, "q_gs", [128, 32], F32)
    q_sc = sb(esq, "q_sc", [128, 32], BF16)
    q_sf = sb(esq, "q_sf", [128, 32], F32)
    q_th = sb(esq, "q_th", [128, 32], F32)
    tstg = sb(esq, "tstg", [128, 2, 512], BF16)

    wgu_tiles = wgu_sl.ap().rearrange("(n p) d -> p n d", p=128)
    wpj_tiles = wpj_sl.ap().rearrange("(n p) h -> p n h", p=128)
    qn, pn = 8, 2
    tp = {"grp": 0}

    def quant_round(src_tiles, r0, nt, d, G, marks):
        wait("gpsimd", sVE, marks["aw_done"])
        wait("gpsimd", sSE, marks["sign"])
        inc("gpsimd", lambda g, s=src_tiles, r0=r0, nt=nt:
            g.dma_start(out=q_bf.ap().rearrange("p (n d) -> p n d", n=nt),
                        in_=s[:, r0:r0 + nt, :]), sGD, 16)
        ld = sGD.n
        wait("scalar", sGD, ld)
        wait("scalar", sVE, marks["ht_mult"])
        inc("scalar", lambda s: s.activation(out=q_sg[:, :], in_=q_bf[:, :], func=AF.Sign), sSE)
        inc("scalar", lambda s: s.activation(out=q_aw[:, :], in_=q_bf[:, :], func=AF.Abs), sSE)
        marks["sign"] = sSE.n
        wait("vector", sGD, ld)
        wait("vector", sTR[0], marks["tpA"])
        wait("vector", sTR[1], marks["tpB"])
        for t in range(nt):
            on("vector", lambda v, t=t, d=d, G=G:
               v.tensor_reduce(q_gs[:, G * t:G * (t + 1)],
                               q_bf.ap()[:, t * d:(t + 1) * d].rearrange("p (g k) -> p g k", g=G),
                               AX.X, ALU.add, apply_absolute_value=True))
        bump("vector", sVE)
        wait("vector", sVE)
        inc("vector", lambda v:
            v.tensor_scalar(q_sc[:, :], q_gs[:, :], 1.0 / 64, 1e-8, ALU.mult, ALU.max), sVE)
        marks["aw_done"] = sVE.n
        wait("vector", sVE)
        inc("vector", lambda v: v.tensor_copy(q_sf[:, :], q_sc[:, :]), sVE)
        wait("vector", sVE)
        inc("vector", lambda v:
            v.tensor_scalar(q_th[:, :], q_sf[:, :], 0.5009765625, None, ALU.mult), sVE)
        wait("vector", sVE)
        wait("vector", sSE, marks["sign"])
        for t in range(nt):
            for gg in range(G):
                c0 = t * d + 64 * gg
                sc0 = G * t + gg
                on("vector", lambda v, c0=c0, sc0=sc0:
                   v.tensor_scalar(q_ht[:, c0:c0 + 64], q_aw[:, c0:c0 + 64],
                                   q_th[:, sc0:sc0 + 1], q_sf[:, sc0:sc0 + 1],
                                   ALU.is_ge, ALU.mult))
        bump("vector", sVE)
        wait("vector", sVE)
        inc("vector", lambda v:
            v.tensor_tensor(q_ht[:, :], q_ht[:, :], q_sg[:, :], ALU.mult), sVE)
        marks["ht_mult"] = sVE.n

    def tpose_group(cols_js, dst_ap_fn, marks):
        pbi = tp["grp"] % 2
        wait("sync", sTS[pbi], 16 * (tp["grp"] // 2))
        wait("sync", sVE, marks["ht_mult"])
        for i4, c0 in enumerate(cols_js):
            inc("sync", lambda sy, c0=c0, i4=i4, pbi=pbi: sy.dma_start_transpose(
                tstg[:, pbi, 128 * i4:128 * (i4 + 1)], q_ht[:, c0:c0 + 128]), sTR[pbi], 16)
        wait("sync", sTR[pbi], sTR[pbi].n)
        inc("sync", lambda sy, pbi=pbi, nw=len(cols_js), f=dst_ap_fn:
            sy.dma_start(out=f(), in_=tstg[:, pbi, 0:128 * nw]), sTS[pbi], 16)
        tp["grp"] += 1

    qmarks = {"ht_mult": 0, "sign": 0, "aw_done": 0, "tpA": 0, "tpB": 0}
    quant_steps = []
    for rr in range((2 * HF // 128) // qn):
        def step_g(rr=rr):
            quant_round(wgu_tiles, qn * rr, qn, 256, 4, qmarks)
            for j in range(2):
                for hg in range(2):
                    cols_js = [(4 * hg + i4) * 256 + 128 * j for i4 in range(4)]
                    js2 = 2 * rr + hg
                    tpose_group(cols_js, lambda js2=js2, j=j: wguc_dram.ap()[js2, :, j, :], qmarks)
            qmarks["tpA"] = sTR[0].n
            qmarks["tpB"] = sTR[1].n
        quant_steps.append(step_g)
    for rr in range((D // 128) // pn):
        def step_p(rr=rr):
            quant_round(wpj_tiles, pn * rr, pn, 1024, 16, qmarks)
            for hj in range(8):
                cols_js = [t_i * 1024 + 128 * hj for t_i in range(2)]
                tpose_group(cols_js,
                            lambda hj=hj, rr=rr: wpjc_dram.ap()[128 * hj:128 * (hj + 1),
                                                                256 * rr:256 * (rr + 1)], qmarks)
            qmarks["tpA"] = sTR[0].n
            qmarks["tpB"] = sTR[1].n
        quant_steps.append(step_p)

    SCB = sSC.n

    wait("scalar", sVE, bias_ready)
    wait("tensor", sVE, bias_ready)
    wait("tensor", sKVL, kv_ld)
    EXB = sEX.n
    YMB = sYM.n
    yev = 0
    for cc in range(4):
        if cc < 2:
            wait("tensor", sKC[cc], 16)
        else:
            wait("tensor", sKC2[cc - 2], 16)
        for m in range(2):
            pp = 2 * cc + m
            sA, sB2 = 2 * m, 2 * m + 1
            hA, hB = 4 * cc + 2 * m, 4 * cc + 2 * m + 1
            for qc in range(4):
                combo = (cc * 2 + m) * 4 + qc
                if quant_steps:
                    quant_steps.pop(0)()
                if combo == 6:
                    wait("sync", sSC, SCB + 512)
                    inc("sync", lambda s: s.dma_start(
                        out=k_cc[:, 0, :, :], in_=kg_view[:, 2, :, :]), sKC2[0], 16)
                if combo == 14:
                    wait("sync", sSC, SCB + 1024)
                    inc("sync", lambda s: s.dma_start(
                        out=k_cc[:, 1, :, :], in_=kg_view[:, 3, :, :]), sKC2[1], 16)
                def y_group(kt):
                    st, sp = (kt == 0), (kt == NKT - 1)
                    gstep = combo * NKT + kt
                    pb = gstep % 4
                    wait("tensor", sEX, EXB + gstep + 1)
                    if kt == 0:
                        wait("tensor", sVE, yev)
                    on("tensor", lambda t, kt=kt, pb=pb, hA=hA, st=st, sp=sp:
                       t.matmul(p_yA[0:32, :], v_full[:, kt, 32 * hA:32 * (hA + 1)],
                                expT[:, pb, 0:512], start=st, stop=sp, tile_position=(0, 0)))
                    on("tensor", lambda t, kt=kt, pb=pb, hB=hB, st=st, sp=sp:
                       t.matmul(p_yB[32:64, :], v_full[:, kt, 32 * hB:32 * (hB + 1)],
                                expT[:, pb, 512:1024], start=st, stop=sp, tile_position=(0, 32)))
                    on("tensor", lambda t, pb=pb, st=st, sp=sp:
                       t.matmul(p_da[64:65, :], ones_col[:, :], expT[:, pb, 0:512],
                                start=st, stop=sp, tile_position=(0, 64)))
                    inc("tensor", lambda t, pb=pb, st=st, sp=sp:
                        t.matmul(p_db[96:97, :], ones_col[:, :], expT[:, pb, 512:1024],
                                 start=st, stop=sp, tile_position=(0, 96)), sYM)

                for kt in range(NKT):
                    gstep = combo * NKT + kt
                    pb = kt % 2
                    pbe = gstep % 4
                    rr_, jj_ = kt // 16, kt % 16
                    wait("tensor", sEX, EXB + gstep - 1)
                    if kt == 0:
                        wait("tensor", sVE, yev)
                    on("tensor", lambda t, sA=sA, cc=cc, rr_=rr_, jj_=jj_, qc=qc, pb=pb:
                       t.matmul(p_sc[:, pb, 0:512],
                                k_cc[32 * sA:32 * (sA + 1), cc % 2, rr_, 128 * jj_:128 * (jj_ + 1)],
                                q_rT[32 * sA:32 * (sA + 1), cc, 512 * qc:512 * (qc + 1)],
                                start=True, stop=True, tile_position=(32 * sA, 0)))
                    inc("tensor", lambda t, sB2=sB2, cc=cc, rr_=rr_, jj_=jj_, qc=qc, pb=pb:
                        t.matmul(p_sc[:, pb, 512:1024],
                                 k_cc[32 * sB2:32 * (sB2 + 1), cc % 2, rr_, 128 * jj_:128 * (jj_ + 1)],
                                 q_rT[32 * sB2:32 * (sB2 + 1), cc, 512 * qc:512 * (qc + 1)],
                                 start=True, stop=True, tile_position=(32 * sB2, 0)), sSC)
                    wait("scalar", sSC, SCB + gstep + 1)
                    wait("scalar", sYM, YMB + gstep - 3)
                    inc("scalar", lambda s, pb=pb, pbe=pbe, pp=pp: s.activation(
                        out=expT[:, pbe, :], in_=p_sc[:, pb, :], func=AF.Exp,
                        bias=biasp[:, pp:pp + 1]), sEX)
                    if kt >= 1:
                        y_group(kt - 1)
                y_group(NKT - 1)
                wait("vector", sYM)
                on("vector", lambda v: v.reciprocal(recrow[0:1, 0:512], p_da[64:65, :]))
                inc("vector", lambda v: v.reciprocal(recrow[0:1, 512:1024], p_db[96:97, :]), sVE)
                wait("tensor", sVE)
                on("tensor", lambda t: t.matmul(p_sc[0:32, 0, 0:512], ones_r32[:, :], recrow[0:1, 0:512],
                                                start=True, stop=True))
                inc("tensor", lambda t: t.matmul(p_sc[32:64, 0, 0:512], ones_r32[:, :], recrow[0:1, 512:1024],
                                                 start=True, stop=True), sTE)
                wait("vector", sTE)
                inc("vector", lambda v: v.tensor_copy(inv_sb[0:64, :], p_sc[0:64, 0, 0:512]), sVE)
                wait("vector", sVE)
                on("vector", lambda v, m=m, cc=cc, qc=qc: v.tensor_tensor(
                    y_rT[64 * m:64 * m + 32, cc, 512 * qc:512 * (qc + 1)],
                    p_yA[0:32, :], inv_sb[0:32, :], ALU.mult))
                inc("vector", lambda v, m=m, cc=cc, qc=qc: v.tensor_tensor(
                    y_rT[64 * m + 32:64 * (m + 1), cc, 512 * qc:512 * (qc + 1)],
                    p_yB[32:64, :], inv_sb[32:64, :], ALU.mult), sVE)
                yev = sVE.n

    wait("gpsimd", sTS[0])
    wait("gpsimd", sTS[1])
    inc("gpsimd", lambda g: g.collective_compute(
        "AllGather", ALU.bypass, replica_groups=[[0, 1, 2, 3, 4, 5, 6, 7]],
        ins=[wguc_dram.ap().opt()], outs=[wgug_dram.ap().opt()]), sCC)
    inc("gpsimd", lambda g: g.collective_compute(
        "AllGather", ALU.bypass, replica_groups=[[0, 1, 2, 3, 4, 5, 6, 7]],
        ins=[wpjc_dram.ap().opt()], outs=[wpjg_dram.ap().opt()]), sCC)
    AGW = sCC.n

    r.emit(nc, "phase_b")
    if limit == "b":
        return nc
    esq.close()
    esb.close()

    eso = ExitStack()
    lnscr = sb(eso, "lnscr_o", [128, D], F32)
    xo = sb(eso, "xo", [128, 2, D], F32)
    ao_sb = sb(eso, "ao_sb", [128, D], F32)
    x2_sb = sb(eso, "x2_sb", [128, 2, D], F32)
    ln3b = sb(eso, "ln3b", [128, 4, D], BF16)
    ln3tp = sb(eso, "ln3tp", [128, 2, 512], BF16)
    p_ao = pst(eso, "p_ao", [128, 2, 512], F32)
    p_l3 = pst(eso, "p_l3", [128, 2, 1024], BF16)

    wait("sync", sTS[0])
    wait("sync", sTS[1])
    res_mark = 0
    ln3ap_mark = 0
    l3tp_prev = 0
    l3ev = {}
    for tb in range(NTB):
        bi = tb % 2
        wait("sync", sVE, res_mark)
        inc("sync", lambda s, tb=tb, bi=bi: s.dma_start(out=xo[:, bi, :], in_=x_rows[:, tb, :]), sXL, 16)
        ldx = sXL.n
        for dc in range(4):
            pbi = dc % 2
            wait("tensor", sSE)
            for cc in range(4):
                fn = lambda t, tb=tb, cc=cc, dc=dc, pbi=pbi: \
                    t.matmul(p_ao[:, pbi, :], y_rT[:, cc, 128 * tb:128 * (tb + 1)],
                             fold[:, cc, 512 * dc:512 * (dc + 1)],
                             start=(cc == 0), stop=(cc == 3))
                if cc == 3:
                    inc("tensor", fn, sSC)
                else:
                    on("tensor", fn)
            wait("scalar", sSC)
            inc("scalar", lambda s, dc=dc, pbi=pbi:
                s.copy(ao_sb[:, 512 * dc:512 * (dc + 1)], p_ao[:, pbi, :]), sSE)
        ao_done = sSE.n
        wait("vector", sSE, ao_done)
        cols = ln_stats([lambda: ao_sb.ap()[:, :]], 0)
        wait("vector", sSE)
        nm, iv = cols[0]
        on("vector", lambda v, nm=nm, iv=iv: v.tensor_scalar(
            lnscr[:, :], ao_sb[:, :], stat[:, nm:nm + 1], stat[:, iv:iv + 1], ALU.add, ALU.mult))
        bump("vector", sVE)
        wait("vector", sVE)
        wait("vector", sXL, ldx)
        wait("vector", sX2S)
        inc("vector", lambda v, bi=bi: v.tensor_tensor(
            x2_sb[:, bi, :], lnscr[:, :], xo[:, bi, :], ALU.add), sVE)
        res_mark = sVE.n
        wait("sync", sVE)
        inc("sync", lambda s, tb=tb, bi=bi: s.dma_start(
            out=x2_dram.ap()[128 * tb:128 * (tb + 1), :], in_=x2_sb[:, bi, :]), sX2S, 16)
        cols = ln_stats([lambda bi=bi: x2_sb.ap()[:, bi, :]], 4)
        wait("vector", sSE)
        nm, iv = cols[0]
        wait("vector", sSC, l3tp_prev)
        inc("vector", lambda v, bi=bi, tb=tb, nm=nm, iv=iv: v.tensor_scalar(
            ln3b[:, tb % 4, :], x2_sb[:, bi, :], stat[:, nm:nm + 1], stat[:, iv:iv + 1],
            ALU.add, ALU.mult), sVE)
        ln3ap_mark = sVE.n
        if tb % 4 == 3:
            tg = tb // 4
            wait("tensor", sVE, ln3ap_mark)
            for db in range(NDB):
                pbi = db % 2
                if db >= 2:
                    wait("tensor", sVE, l3ev[(tg, db - 2)])
                for tb4 in range(4):
                    fn = lambda t, tb4=tb4, db=db, pbi=pbi: \
                        t.transpose(p_l3[:, pbi, 128 * tb4:128 * (tb4 + 1)],
                                    ln3b[:, tb4, 128 * db:128 * (db + 1)], identb[:, :])
                    if tb4 == 3:
                        inc("tensor", fn, sSC)
                    else:
                        on("tensor", fn)
                wait("vector", sSC)
                if db >= 2 or tg > 0:
                    wait("vector", sL3S)
                on("vector", lambda v, db=db, pbi=pbi: v.tensor_copy(ln3tp[:, pbi, :], p_l3[:, pbi, 0:512]))
                bump("vector", sVE)
                l3ev[(tg, db)] = sVE.n
                wait("sync", sVE, l3ev[(tg, db)])
                inc("sync", lambda s, db=db, tg=tg, pbi=pbi: s.dma_start(
                    out=ln3T_dram.ap()[db, :, 512 * tg:512 * (tg + 1)], in_=ln3tp[:, pbi, :]), sL3S, 16)
            l3tp_prev = sSC.n

    r.emit(nc, "phase_o")
    if limit == "o":
        return nc
    eso.close()
    es1.close()

    esg = ExitStack()
    lnscr = sb(esg, "lnscr_g", [128, D], F32)
    invrow = sb(esg, "invrow", [1, 512], F32)
    x2r = sb(esg, "x2r", [128, 2, D], F32)
    ln3T = sb(esg, "ln3T", [128, NDB, 512], BF16)
    hT = sb(esg, "hT", [128, 64, 512], BF16)
    wguf_g = sb(esg, "wguf_g", [128, 2, NDB, 512], BF16)
    wguf_u = sb(esg, "wguf_u", [128, 2, NDB, 512], BF16)
    wpjf = sb(esg, "wpjf", [128, 2, 1024], BF16)
    sgf = sb(esg, "sgf", [128, 512], F32)
    sqb = sb(esg, "sqb", [128, 2, 512], BF16)
    out_sb = sb(esg, "out_sb", [128, 2, D], BF16)
    ffn_sb = sb(esg, "ffn_sb", [128, 4, D], BF16)
    p_big = pst(esg, "p_big", [128, 8, 512], F32)

    wait("sync", sCC, AGW)
    wait("sync", sL3S)
    wait("sync", sX2S)
    x2_rows = x2_dram.ap().rearrange("(n p) d -> p n d", p=128)
    ln3T_view = ln3T_dram.ap().rearrange("db p t -> p db t")
    gu_cons = {}
    pj_cons = {}
    x2_mark = 0
    l3_mark = 0
    guend_mark = 0
    pjlast_mark = 0
    marks_h = {}
    marks_silu = {}
    marks_ss = {}
    marks_sq = {}
    pjev_mark = 0
    ln4_pending = []
    gmk = {"x2": 0}

    def ln4_step(job):
        sub_, tb4 = job
        n_t = 4 * sub_ + tb4
        bi = n_t % 2
        wait("sync", sVE, gmk["x2"])
        inc("sync", lambda s, n_t=n_t, bi=bi: s.dma_start(
            out=x2r[:, bi, :], in_=x2_rows[:, n_t, :]), sX2, 16)
        ldx2 = sX2.n
        cols = ln_stats([lambda tb4=tb4: ffn_sb.ap()[:, tb4, :]], 16)
        wait("vector", sSE)
        nm, iv = cols[0]
        on("vector", lambda v, tb4=tb4, nm=nm, iv=iv: v.tensor_scalar(
            lnscr[:, :], ffn_sb[:, tb4, :], stat[:, nm:nm + 1], stat[:, iv:iv + 1],
            ALU.add, ALU.mult))
        bump("vector", sVE)
        wait("vector", sVE)
        wait("vector", sX2, ldx2)
        if n_t >= 2:
            wait("vector", sOUT2[bi], 16 * ((n_t - 2) // 2 + 1))
        inc("vector", lambda v, bi=bi: v.tensor_tensor(
            out_sb[:, bi, :], lnscr[:, :], x2r[:, bi, :], ALU.add), sVE)
        gmk["x2"] = sVE.n
        wait("gpsimd", sVE)
        inc("gpsimd", lambda g, n_t=n_t, bi=bi: g.dma_start(
            out=y_out.ap()[128 * n_t:128 * (n_t + 1), :], in_=out_sb[:, bi, :]), sOUT2[bi], 16)

    for sub in range(NSUB):
        wait("sync", sSC, guend_mark)
        inc("sync", lambda s, sub=sub: s.dma_start(
            out=ln3T[:, :, :], in_=ln3T_view[:, :, 512 * sub:512 * (sub + 1)]), sL3, 16)
        l3_mark = sL3.n
        wait("tensor", sL3, l3_mark)
        wait("vector", sTE, pjlast_mark)
        wait("tensor", sVE, pjev_mark)
        for js in range(16):
            bi = js % 2
            gjs = 16 * sub + js
            wait("sync", sSC, gu_cons.get(gjs - 2, 0))
            inc("sync", lambda s, js=js, bi=bi: s.dma_start(
                out=wguf_g.ap()[:, bi, :, :].rearrange("p (r dbl) hl -> p r dbl hl", r=8),
                in_=wgug_dram.ap()[:, js, :, :, :].rearrange("r p dbl hl -> p r dbl hl")),
            sFWs[bi], 16)
            inc("sync", lambda s, js=js, bi=bi: s.dma_start(
                out=wguf_u.ap()[:, bi, :, :].rearrange("p (r dbl) hl -> p r dbl hl", r=8),
                in_=wgug_dram.ap()[:, 16 + js, :, :, :].rearrange("r p dbl hl -> p r dbl hl")),
            sFWs[bi], 16)
            ldw = sFWs[bi].n
            wait("tensor", sFWs[bi], ldw)
            for jj in range(4):
                jb = 4 * js + jj
                gjb = 64 * sub + jb
                wait("tensor", sSE, marks_silu.get(gjb - 2, 0))
                wait("tensor", sVE, marks_h.get(gjb - 2, 0))
                for db in range(NDB):
                    on("tensor", lambda t, bi=bi, db=db, jj=jj:
                       t.matmul(p_big[:, jj % 2, :], wguf_g[:, bi, db, 128 * jj:128 * (jj + 1)],
                                ln3T[:, db, :], start=(db == 0), stop=(db == NDB - 1)))
                for db in range(NDB):
                    fn = lambda t, bi=bi, db=db, jj=jj: \
                        t.matmul(p_big[:, 2 + jj % 2, :], wguf_u[:, bi, db, 128 * jj:128 * (jj + 1)],
                                 ln3T[:, db, :], start=(db == 0), stop=(db == NDB - 1))
                    if db == NDB - 1:
                        inc("tensor", fn, sSC)
                    else:
                        on("tensor", fn)
                gu_mm = sSC.n
                wait("scalar", sSC, gu_mm)
                wait("scalar", sVE, marks_h.get(gjb - 1, 0))
                inc("scalar", lambda s, jj=jj: s.activation(
                    out=sgf[:, :], in_=p_big[:, jj % 2, :], func=AF.Silu), sSE)
                marks_silu[gjb] = sSE.n
                wait("vector", sSE, marks_silu[gjb])
                inc("vector", lambda v, jb=jb, jj=jj: v.tensor_tensor(
                    hT[:, jb, :], sgf[:, :], p_big[:, 2 + jj % 2, :], ALU.mult), sVE)
                marks_h[gjb] = sVE.n
                wait("scalar", sVE, marks_h[gjb])
                wait("scalar", sQ2, marks_ss.get(gjb - 2, 0))
                inc("scalar", lambda s, jb=jb: s.activation(
                    out=sqb[:, jb % 2, :], in_=hT[:, jb, :], func=AF.Square), sSE)
                marks_sq[gjb] = sSE.n
                if jb >= 1:
                    wait("tensor", sSE, marks_sq[gjb - 1])
                    inc("tensor", lambda t, jb=jb: t.matmul(
                        p_big[0:1, 4, :], ones_col[:, :], sqb[:, (jb - 1) % 2, :],
                        start=(jb == 1), stop=False), sQ2)
                    marks_ss[gjb - 1] = sQ2.n
            gu_cons[gjs] = sSC.n
            if ln4_pending:
                ln4_step(ln4_pending.pop(0))
        wait("tensor", sSE, marks_sq[64 * sub + 63])
        inc("tensor", lambda t: t.matmul(
            p_big[0:1, 4, :], ones_col[:, :], sqb[:, 63 % 2, :],
            start=False, stop=True), sQ2)
        marks_ss[64 * sub + 63] = sQ2.n
        guend_mark = sSC.n
        wait("vector", sQ2)
        inc("vector", lambda v: v.tensor_scalar(
            invrow[0:1, :], p_big[0:1, 4, :], 1.0 / HF, RMS_EPS, ALU.mult, ALU.add), sVE)
        wait("scalar", sVE)
        inc("scalar", lambda s: s.activation(out=invrow[0:1, :], in_=invrow[0:1, :], func=AF.Ln), sSE)
        wait("scalar", sSE)
        inc("scalar", lambda s: s.activation(out=invrow[0:1, :], in_=invrow[0:1, :],
                                             func=AF.Exp, scale=-0.5), sSE)
        wait("tensor", sSE)
        for w4 in range(4):
            fn = lambda t, w4=w4: t.matmul(
                p_big[:, 5, w4:w4 + 1], invrow[0:1, 128 * w4:128 * (w4 + 1)], ones_colF[0:1, :],
                start=True, stop=True)
            if w4 == 3:
                inc("tensor", fn, sTE)
            else:
                on("tensor", fn)
        wait("vector", sTE)
        inc("vector", lambda v: v.tensor_copy(inv_t[:, :], p_big[:, 5, 0:4]), sVE)
        invt_done = sVE.n
        for pss in range(2):
            for hb in range(64):
                bi = hb % 2
                ghb = (2 * sub + pss) * 64 + hb
                wait("sync", sTE, pj_cons.get(ghb - 2, 0))
                inc("sync", lambda s, pss=pss, hb=hb, bi=bi: s.dma_start(
                    out=wpjf[:, bi, :],
                    in_=wpjg_dram.ap()[128 * hb:128 * (hb + 1), 1024 * pss:1024 * (pss + 1)]),
                sPJ[bi], 16)
                ldp = sPJ[bi].n
                wait("tensor", sPJ[bi], ldp)
                if hb == 0:
                    wait("tensor", sVE, invt_done if pss == 0 else sVE.n)
                for tb4 in range(4):
                    for dch in range(2):
                        st, sp = (hb == 0), (hb == 63)
                        fn = lambda t, tb4=tb4, dch=dch, hb=hb, bi=bi, st=st, sp=sp: \
                            t.matmul(p_big[:, 2 * tb4 + dch, :],
                                     hT[:, hb, 128 * tb4:128 * (tb4 + 1)],
                                     wpjf[:, bi, 512 * dch:512 * (dch + 1)],
                                     start=st, stop=sp)
                        if tb4 == 3 and dch == 1:
                            inc("tensor", fn, sTE)
                        else:
                            on("tensor", fn)
                pj_cons[ghb] = sTE.n
            pj_mm = sTE.n
            wait("vector", sTE, pj_mm)
            for tb4 in range(4):
                for dch in range(2):
                    dc = 2 * pss + dch
                    on("vector", lambda v, tb4=tb4, dch=dch, dc=dc: v.tensor_scalar(
                        ffn_sb[:, tb4, 512 * dc:512 * (dc + 1)], p_big[:, 2 * tb4 + dch, :],
                        inv_t[:, tb4:tb4 + 1], None, ALU.mult))
            bump("vector", sVE)
            pjev_mark = sVE.n
        pjlast_mark = sTE.n
        for tb4 in range(4):
            ln4_pending.append((sub, tb4))
        if sub == NSUB - 1:
            while ln4_pending:
                ln4_step(ln4_pending.pop(0))

    wait("gpsimd", sOUT2[0])
    wait("gpsimd", sOUT2[1])
    r.emit(nc, "phase_g")
    esg.close()
    es0.close()
    return nc


_NC_CACHE = {}


def kernel(**inputs):
    x = np.asarray(inputs["L0_attn_x"], dtype=np.float32)
    wq = np.ascontiguousarray(np.asarray(inputs["L0_attn_Wq"], dtype=np.float32))
    wk = np.ascontiguousarray(np.asarray(inputs["L0_attn_Wk"], dtype=np.float32))
    wv = np.ascontiguousarray(np.asarray(inputs["L0_attn_Wv"], dtype=np.float32))
    wo = np.ascontiguousarray(np.asarray(inputs["L0_attn_Wo"], dtype=np.float32))
    wdown = np.ascontiguousarray(np.asarray(inputs["L0_attn_Wdown"], dtype=np.float32))
    wup = np.ascontiguousarray(np.asarray(inputs["L0_attn_Wup"], dtype=np.float32))
    wgu = np.asarray(inputs["L0_ffn_w_gate_up"], dtype=np.float32)
    wpj = np.asarray(inputs["L0_ffn_w_proj"], dtype=np.float32)

    if "nc" not in _NC_CACHE:
        _NC_CACHE["nc"] = build_nc()
    nc = _NC_CACHE["nc"]

    in_maps = []
    for c in range(8):
        b, r0 = c // 4, (c % 4) * TOK
        in_maps.append({
            "x": np.ascontiguousarray(x[b, r0:r0 + TOK, :]),
            "Wq": wq, "Wk": wk, "Wv": wv, "Wo": wo,
            "Wdown": wdown, "Wup": wup,
            "wgu_sl": np.ascontiguousarray(wgu[:, 256 * c:256 * (c + 1)]),
            "wpj_sl": np.ascontiguousarray(wpj[:, 1024 * c:1024 * (c + 1)]),
        })
    res = run_bass_kernel_spmd(nc, in_maps, core_ids=list(range(8)))
    out = np.empty((2, T, D), dtype=np.float32)
    for c in range(8):
        b, r0 = c // 4, (c % 4) * TOK
        out[b, r0:r0 + TOK, :] = res.results[c]["y"]
    return out


# version 11 (final)
# speedup vs baseline: 1.1064x; 1.1064x over previous
import numpy as np
from contextlib import ExitStack

import concourse.bass as bass
import concourse.mybir as mybir
from concourse.bass_utils import run_bass_kernel_spmd

F32 = mybir.dt.float32
BF16 = mybir.dt.bfloat16
AF = mybir.ActivationFunctionType
ALU = mybir.AluOpType
AX = mybir.AxisListType

D = 2048
NH = 16
RK = 32
HF = 8192
TOK = 2048
T = 8192
NTB = TOK // 128
NDB = D // 128
NKT = T // 128
SCALE = 1.0 / float(np.sqrt(RK))
LN_EPS = 1e-5
RMS_EPS = float(np.finfo(np.float32).eps)
SUB = 512
NSUB = TOK // SUB


class Rec:
    ENGS = ("tensor", "vector", "scalar", "gpsimd", "sync")

    def __init__(self):
        self.q = {e: [] for e in self.ENGS}

    def emit(self, nc, name):
        with nc.Block(name) as blk:
            for e in self.ENGS:
                fns = self.q[e]
                if fns:
                    getattr(blk, e)(lambda eng, fns=fns: [f(eng) for f in fns])
        self.q = {e: [] for e in self.ENGS}


def build_nc(limit=None, debug=False):
    nc = bass.Bass("TRN2", target_bir_lowering=False)

    x_in = nc.dram_tensor("x", [TOK, D], F32, kind="ExternalInput")
    wq_in = nc.dram_tensor("Wq", [D, D], F32, kind="ExternalInput")
    wk_in = nc.dram_tensor("Wk", [D, D], F32, kind="ExternalInput")
    wv_in = nc.dram_tensor("Wv", [D, D], F32, kind="ExternalInput")
    wo_in = nc.dram_tensor("Wo", [D, D], F32, kind="ExternalInput")
    wdown_in = nc.dram_tensor("Wdown", [128, RK], F32, kind="ExternalInput")
    wup_in = nc.dram_tensor("Wup", [RK, 128], F32, kind="ExternalInput")
    wgu_sl = nc.dram_tensor("wgu_sl", [2 * HF, 256], F32, kind="ExternalInput")
    wpj_sl = nc.dram_tensor("wpj_sl", [D, 1024], F32, kind="ExternalInput")
    y_out = nc.dram_tensor("y", [TOK, D], F32, kind="ExternalOutput")

    ik = "ExternalOutput" if debug else "Internal"
    x2_dram = nc.dram_tensor("x2_dram", [TOK, D], F32, kind=ik)
    kc_dram = nc.dram_tensor("kc_dram", [512, TOK], BF16)
    kg_dram = nc.dram_tensor("kg_dram", [2048, TOK], BF16)
    vc_dram = nc.dram_tensor("vc_dram", [TOK, 512], BF16)
    vg_dram = nc.dram_tensor("vg_dram", [T, 512], BF16)
    kmax_dram = nc.dram_tensor("kmax_dram", [512], F32)
    kmaxg_dram = nc.dram_tensor("kmaxg_dram", [4 * 512], F32)
    wguc_dram = nc.dram_tensor("wguc_dram", [32, 128, 2, 512], BF16)
    wgug_dram = nc.dram_tensor("wgug_dram", [8, 32, 128, 2, 512], BF16, addr_space="Shared")
    wpjc_dram = nc.dram_tensor("wpjc_dram", [1024, D], BF16)
    wpjg_dram = nc.dram_tensor("wpjg_dram", [HF, D], BF16, addr_space="Shared")
    ln3T_dram = nc.dram_tensor("ln3T_dram", [NDB, 128, TOK], BF16, kind=ik)
    if debug:
        dbg_kg = nc.dram_tensor("dbg_kg", [2048, TOK], BF16, kind="ExternalOutput")
        dbg_vg = nc.dram_tensor("dbg_vg", [T, 512], BF16, kind="ExternalOutput")
        dbg_kc = nc.dram_tensor("dbg_kc", [512, TOK], BF16, kind="ExternalOutput")
        dbg_vc = nc.dram_tensor("dbg_vc", [TOK, 512], BF16, kind="ExternalOutput")
        dbg_q = nc.dram_tensor("dbg_q", [128, 4, TOK], BF16, kind="ExternalOutput")
        dbg_y = nc.dram_tensor("dbg_y", [128, 4, TOK], BF16, kind="ExternalOutput")
        dbg_fold = nc.dram_tensor("dbg_fold", [128, 4, D], BF16, kind="ExternalOutput")
        dbg_wqr = nc.dram_tensor("dbg_wqr", [128, NDB, 512], BF16, kind="ExternalOutput")
        dbg_bias = nc.dram_tensor("dbg_bias", [128, 8], F32, kind="ExternalOutput")
    ident_dram = nc.inline_tensor(np.eye(128, dtype=np.float32), name="ident_f32")

    es0 = ExitStack()

    def sb(stk, name, shape, dtype):
        return stk.enter_context(nc.sbuf_tensor(name, shape, dtype))

    def pst(stk, name, shape, dtype=F32):
        return stk.enter_context(nc.psum_tensor(name, shape, dtype))

    class Ctr:
        def __init__(self, name):
            self.sem = es0.enter_context(nc.semaphore(name))
            self.n = 0

    r = Rec()

    def on(eng, fn):
        r.q[eng].append(fn)

    def inc(eng, mk, ctr, k=1):
        ctr.n += k
        r.q[eng].append(lambda e, mk=mk, s=ctr.sem, k=k: mk(e).then_inc(s, k))

    def bump(eng, ctr, k=1):
        ctr.n += k
        last = r.q[eng].pop()
        r.q[eng].append(lambda e, last=last, sm=ctr.sem, k=k: last(e).then_inc(sm, k))

    def wait(eng, ctr, n=None):
        n0 = ctr.n if n is None else n
        if n0 > 0:
            r.q[eng].append(lambda e, s=ctr.sem, n=n0: e.wait_ge(s, n))

    identb = sb(es0, "identb", [128, 128], BF16)
    ones_col = sb(es0, "ones_col", [128, 1], BF16)
    ones_colF = sb(es0, "ones_colF", [128, 1], F32)
    ones_r32 = sb(es0, "ones_r32", [1, 32], F32)
    ones_r128 = sb(es0, "ones_r128", [1, 128], F32)
    biasp = sb(es0, "biasp", [128, 8], F32)
    wdownq = sb(es0, "wdownq", [128, RK], BF16)
    wdownk = sb(es0, "wdownk", [128, RK], BF16)
    wupT = sb(es0, "wupT", [128, RK], BF16)
    nrm = sb(es0, "nrm", [128, 16], F32)
    brow = sb(es0, "brow", [1, 32], F32)
    inv_sb = sb(es0, "inv_sb", [128, 512], F32)
    stat = sb(es0, "stat", [128, 32], F32)
    inv_t = sb(es0, "inv_t", [128, 4], F32)
    wup_sb0 = sb(es0, "wup_sb0", [RK, 128], BF16)
    kmax_acc = sb(es0, "kmax_acc", [128, 8], F32)
    kmxt = sb(es0, "kmxt", [128, 16], F32)

    sLD = Ctr("sLD")
    sCONST = Ctr("sCONST")
    sWT = [Ctr("sWT0"), Ctr("sWT1")]
    sWO = [Ctr("sWO0"), Ctr("sWO1")]
    sTS = [Ctr("sTS0"), Ctr("sTS1")]
    sKV = Ctr("sKV")
    sKVL = Ctr("sKVL")
    sX2S = Ctr("sX2S")
    sL3S = Ctr("sL3S")
    sL3 = Ctr("sL3")
    sX2 = Ctr("sX2")
    sDBG = Ctr("sDBG")
    sOUT2 = [Ctr("sOUT0"), Ctr("sOUT1")]
    sFWs = [Ctr("sFWa"), Ctr("sFWb")]
    sPJ = [Ctr("sPJa"), Ctr("sPJb")]
    sKC = [Ctr("sKC0"), Ctr("sKC1")]
    sKC2 = [Ctr("sKC2a"), Ctr("sKC2b")]
    sTR = [Ctr("sTR0"), Ctr("sTR1")]
    sKM = Ctr("sKM")
    sKM2 = Ctr("sKM2")
    sGD = Ctr("sGD")
    sTE = Ctr("sTE")
    sVE = Ctr("sVE")
    sSE = Ctr("sSE")
    sCC = Ctr("sCC")
    sOUT = Ctr("sOUT")
    sSC = Ctr("sSC")
    sEX = Ctr("sEX")
    sYM = Ctr("sYM")
    sQ2 = Ctr("sQ2")
    sXL = Ctr("sXL")
    sFW = Ctr("sFW")

    es1 = ExitStack()
    q_rT = sb(es1, "q_rT", [128, 4, TOK], BF16)
    fold = sb(es1, "fold", [128, 4, D], BF16)
    y_rT = sb(es1, "y_rT", [128, 4, TOK], BF16)

    def ln_stats(tiles, scol):
        nt = len(tiles)
        cols = []
        wait("vector", sVE)
        for i, ap_fn in enumerate(tiles):
            c0 = scol + 4 * i
            on("vector", lambda v, f=ap_fn, c0=c0: v.reduce_sum(stat[:, c0:c0 + 1], f(), AX.X))
            on("vector", lambda v, f=ap_fn, c0=c0: v.scalar_tensor_tensor(
                lnscr[:, :], f(), 1.0, f(), ALU.mult, ALU.mult,
                accum_out=stat[:, c0 + 1:c0 + 2]))
            bump("vector", sVE)
            wait("vector", sVE)
            cols.append((c0 + 2, c0 + 3))
        for i in range(nt):
            c0 = scol + 4 * i
            on("vector", lambda v, c0=c0: v.tensor_scalar(
                stat[:, c0 + 2:c0 + 3], stat[:, c0:c0 + 1], -1.0 / D, None, ALU.mult))
        bump("vector", sVE)
        wait("vector", sVE)
        for i in range(nt):
            c0 = scol + 4 * i
            on("vector", lambda v, c0=c0: v.tensor_tensor(
                stat[:, c0:c0 + 1], stat[:, c0 + 2:c0 + 3], stat[:, c0 + 2:c0 + 3], ALU.mult))
        bump("vector", sVE)
        wait("vector", sVE)
        for i in range(nt):
            c0 = scol + 4 * i
            on("vector", lambda v, c0=c0: v.tensor_scalar(
                stat[:, c0 + 3:c0 + 4], stat[:, c0 + 1:c0 + 2], 1.0 / D, LN_EPS, ALU.mult, ALU.add))
        bump("vector", sVE)
        wait("vector", sVE)
        for i in range(nt):
            c0 = scol + 4 * i
            on("vector", lambda v, c0=c0: v.tensor_tensor(
                stat[:, c0 + 3:c0 + 4], stat[:, c0 + 3:c0 + 4], stat[:, c0:c0 + 1], ALU.subtract))
        bump("vector", sVE)
        ve_stats = sVE.n
        wait("scalar", sVE, ve_stats)
        for i in range(nt):
            c0 = scol + 4 * i
            fn = lambda s, c0=c0: s.activation(
                out=stat[:, c0 + 3:c0 + 4], in_=stat[:, c0 + 3:c0 + 4], func=AF.Ln)
            if i == nt - 1:
                inc("scalar", fn, sSE)
            else:
                on("scalar", fn)
        wait("scalar", sSE)
        for i in range(nt):
            c0 = scol + 4 * i
            fn = lambda s, c0=c0: s.activation(
                out=stat[:, c0 + 3:c0 + 4], in_=stat[:, c0 + 3:c0 + 4], func=AF.Exp, scale=-0.5)
            if i == nt - 1:
                inc("scalar", fn, sSE)
            else:
                on("scalar", fn)
        return cols

    on("gpsimd", lambda g: g.memset(ones_col[:, :], 1.0))
    on("gpsimd", lambda g: g.memset(ones_colF[:, :], 1.0))
    on("gpsimd", lambda g: g.memset(ones_r32[:, :], 1.0))
    on("gpsimd", lambda g: g.memset(ones_r128[:, :], 1.0))
    inc("gpsimd", lambda g: g.dma_start(out=identb[:, :], in_=ident_dram.ap()[:, :]), sCONST, 16)
    inc("gpsimd", lambda g: g.dma_start(out=wdownk[:, :], in_=wdown_in.ap()[:, :]), sCONST, 16)
    inc("gpsimd", lambda g: g.dma_start(out=wup_sb0[:, :], in_=wup_in.ap()[:, :]), sCONST, 16)

    esFA = ExitStack()
    wqr = sb(esFA, "wqr", [128, NDB, 512], BF16)
    wkr = sb(esFA, "wkr", [128, NDB, 512], BF16)
    wvr = sb(esFA, "wvr", [128, NDB, 512], BF16)

    esf = ExitStack()
    wtile = sb(esf, "wtile", [128, 2, 3, D], BF16)
    wotile = sb(esf, "wotile", [128, 2, NDB, 128], BF16)
    wstrip = sb(esf, "wstrip", [128, 2, D], BF16)
    p_f = pst(esf, "p_f", [128, 3, 128], F32)
    p_w = pst(esf, "p_w", [128, 512], BF16)
    p_o = [pst(esf, "p_o%d" % i, [128, 512], F32) for i in range(4)]
    p_s = pst(esf, "p_s", [128, RK], BF16)

    wait("vector", sCONST, 48)
    inc("vector", lambda v: v.tensor_scalar(wdownq[:, :], wdownk[:, :], SCALE, None, ALU.mult), sVE)
    ve_wd = sVE.n
    wait("tensor", sCONST, 48)
    inc("tensor", lambda t: t.transpose(p_s[:, :], wup_sb0[:, :], identb[0:RK, 0:RK]), sTE)
    wait("vector", sTE)
    inc("vector", lambda v: v.tensor_copy(wupT[:, :], p_s[:, :]), sVE)
    ve_wu = sVE.n

    if limit == "f1":
        r.emit(nc, "phase_f")
        return nc
    wq_rows = wq_in.ap().rearrange("(n p) d -> p n d", p=128)
    wk_rows = wk_in.ap().rearrange("(n p) d -> p n d", p=128)
    wv_rows = wv_in.ap().rearrange("(n p) d -> p n d", p=128)
    x_rows = x_in.ap().rearrange("(n p) d -> p n d", p=128)
    mm_done = {}
    ev_done = {}
    for h in range(NH):
        bi = h % 2
        if h >= 2:
            wait("gpsimd", sTE, mm_done[h - 2])
        inc("gpsimd", lambda g, h=h, bi=bi: g.dma_start(out=wtile[:, bi, 0, :], in_=wq_rows[:, h, :]), sWT[bi], 16)
        inc("gpsimd", lambda g, h=h, bi=bi: g.dma_start(out=wtile[:, bi, 1, :], in_=wk_rows[:, h, :]), sWT[bi], 16)
        inc("gpsimd", lambda g, h=h, bi=bi: g.dma_start(out=wtile[:, bi, 2, :], in_=wv_rows[:, h, :]), sWT[bi], 16)
        ldh = sWT[bi].n
        if limit == "f2a":
            mm_done[h] = 0
            for dbg in range(4):
                ev_done[(h, dbg)] = 0
            continue
        wait("tensor", sWT[bi], ldh)
        wait("tensor", sVE, max(ve_wd, ve_wu))
        for dbg in range(4):
            wait("tensor", sVE, ev_done.get((h, dbg - 1), ev_done.get((h - 1, 3), 0)))
            for db in range(4 * dbg, 4 * dbg + 4):
                g4 = db % 4
                for w_i in range(3):
                    wd = wdownq if w_i == 0 else wdownk
                    fn = lambda t, bi=bi, db=db, g4=g4, w_i=w_i, wd=wd: \
                        t.matmul(p_f[:, w_i, 32 * g4:32 * (g4 + 1)],
                                 wtile[:, bi, w_i, 128 * db:128 * (db + 1)], wd[:, :],
                                 start=True, stop=True)
                    if db == 4 * dbg + 3 and w_i == 2:
                        inc("tensor", fn, sTE)
                    else:
                        on("tensor", fn)
            if limit == "f2b":
                ev_done[(h, dbg)] = 0
                continue
            wait("vector", sTE)
            for w_i, dstw in ((0, wqr), (1, wkr), (2, wvr)):
                for g4 in range(4):
                    on("vector", lambda v, dstw=dstw, w_i=w_i, dbg=dbg, h=h, g4=g4:
                       v.tensor_copy(dstw.ap()[:, 4 * dbg + g4, 32 * h:32 * (h + 1)],
                                     p_f[:, w_i, 32 * g4:32 * (g4 + 1)]))
            bump("vector", sVE)
            ev_done[(h, dbg)] = sVE.n
        mm_done[h] = sTE.n

    if limit in ("f2", "f2a", "f2b", "skipq_f2", "f2d"):
        if debug and limit in ("skipq_f2", "f2d"):
            wait("sync", sVE)
            inc("sync", lambda sy: sy.dma_start(out=dbg_wqr.ap()[:, :, :], in_=wqr[:, :, :]), sDBG, 16)
            wait("sync", sDBG)
        r.emit(nc, "phase_f")
        return nc
    strip_tposed = {}
    foldmm = {}
    foldev = {}
    for c in range(4):
        for m in range(4):
            h = 4 * c + m
            bi = h % 2
            if h >= 2:
                wait("gpsimd", sSC, strip_tposed[h - 2])
            inc("gpsimd", lambda g, h=h, bi=bi: g.dma_start(
                out=wotile[:, bi, :, :],
                in_=wo_in.ap()[:, 128 * h:128 * (h + 1)].rearrange("(n p) d -> p n d", p=128)), sWO[bi], 16)
            ldo = sWO[bi].n
            wait("tensor", sWO[bi], ldo)
            for dbg in range(4):
                wait("tensor", sVE)
                for db in range(4 * dbg, 4 * dbg + 4):
                    fn = lambda t, bi=bi, db=db: \
                        t.transpose(p_w[:, 128 * (db % 4):128 * (db % 4 + 1)],
                                    wotile[:, bi, db, :], identb[:, :])
                    if db == 4 * dbg + 3:
                        inc("tensor", fn, sSC)
                    else:
                        on("tensor", fn)
                wait("vector", sSC)
                if h >= 2:
                    wait("vector", sTE, foldmm[h - 2])
                on("vector", lambda v, bi=bi, dbg=dbg:
                   v.tensor_copy(wstrip[:, bi, 512 * dbg:512 * (dbg + 1)], p_w[:, :]))
                bump("vector", sVE)
            strip_tposed[h] = sSC.n
            wait("tensor", sVE)
            if m == 0 and c > 0:
                wait("tensor", sVE, foldev[c - 1])
            for dc in range(4):
                fn = lambda t, m=m, bi=bi, dc=dc: \
                    t.matmul(p_o[dc][32 * m:32 * (m + 1), :], wupT[:, :],
                             wstrip[:, bi, 512 * dc:512 * (dc + 1)],
                             start=True, stop=True, tile_position=(0, 32 * m))
                if dc == 3:
                    inc("tensor", fn, sTE)
                else:
                    on("tensor", fn)
            foldmm[h] = sTE.n
            if m == 3:
                wait("vector", sTE, foldmm[h])
                for dc in range(4):
                    on("vector", lambda v, c=c, dc=dc:
                       v.tensor_copy(fold[:, c, 512 * dc:512 * (dc + 1)], p_o[dc][:, :]))
                bump("vector", sVE)
                foldev[c] = sVE.n

    r.emit(nc, "phase_f")
    if limit == "f":
        return nc
    esf.close()

    esa = ExitStack()
    lnscr = sb(esa, "lnscr_a", [128, D], F32)
    xln = sb(esa, "xln", [128, 4, D], F32)
    ln1b = sb(esa, "ln1b", [128, 4, D], BF16)
    ln1T = sb(esa, "ln1T", [128, NDB, 512], BF16)
    kcon = sb(esa, "kcon", [128, 4, 512], BF16)
    vcon = sb(esa, "vcon", [128, 4, 512], BF16)
    p_ln = pst(esa, "p_ln", [128, 2, 1024], BF16)
    p_qk = pst(esa, "p_qk", [128, 2, 512], F32)
    p_v = pst(esa, "p_v", [128, 2, 512], F32)

    if debug:
        wait("sync", sVE)
        inc("sync", lambda sy: sy.dma_start(out=dbg_wqr.ap()[:, :, :], in_=wqr[:, :, :]), sDBG, 16)
    apply_mark = 0
    tpose_mark = 0
    qkv_mark = 0
    kv_dma_mark = 0
    evq = {}
    evv = {}
    for tc in range(4):
        wait("sync", sVE, apply_mark)
        for tb4 in range(4):
            ti = 4 * tc + tb4
            inc("sync", lambda s, ti=ti, tb4=tb4: s.dma_start(out=xln[:, tb4, :], in_=x_rows[:, ti, :]), sXL, 16)
        ldx = sXL.n
        wait("vector", sXL, ldx)
        wait("vector", sSC, tpose_mark)
        cols = ln_stats([(lambda tb4=tb4: xln.ap()[:, tb4, :]) for tb4 in range(4)], 0)
        wait("vector", sSE)
        for tb4 in range(4):
            nm, iv = cols[tb4]
            on("vector", lambda v, tb4=tb4, nm=nm, iv=iv:
               v.tensor_scalar(ln1b[:, tb4, :], xln[:, tb4, :],
                               stat[:, nm:nm + 1], stat[:, iv:iv + 1], ALU.add, ALU.mult))
        bump("vector", sVE)
        apply_mark = sVE.n
        wait("tensor", sVE, apply_mark)
        wait("tensor", sSC, qkv_mark)
        for db in range(NDB):
            pbi = db % 2
            if db >= 2:
                wait("tensor", sVE, evq.get(("lt", tc, db - 2), 0))
            for tb4 in range(4):
                fn = lambda t, tb4=tb4, db=db, pbi=pbi: \
                    t.transpose(p_ln[:, pbi, 128 * tb4:128 * (tb4 + 1)],
                                ln1b[:, tb4, 128 * db:128 * (db + 1)], identb[:, :])
                if tb4 == 3:
                    inc("tensor", fn, sSC)
                else:
                    on("tensor", fn)
            wait("vector", sSC)
            on("vector", lambda v, db=db, pbi=pbi: v.tensor_copy(ln1T[:, db, :], p_ln[:, pbi, 0:512]))
            bump("vector", sVE)
            evq[("lt", tc, db)] = sVE.n
        tpose_mark = sSC.n
        lt_done = sVE.n
        wait("tensor", sVE, lt_done)
        for qk in range(2):
            wmat = wqr if qk == 0 else wkr
            for cc in range(4):
                pbi = cc % 2
                wait("tensor", sVE, evq.get(("qk", tc, qk * 4 + cc - 2), 0))
                for db in range(NDB):
                    fn = lambda t, wmat=wmat, cc=cc, db=db, pbi=pbi: \
                        t.matmul(p_qk[:, pbi, :], wmat[:, db, 128 * cc:128 * (cc + 1)],
                                 ln1T[:, db, :], start=(db == 0), stop=(db == NDB - 1))
                    if db == NDB - 1:
                        inc("tensor", fn, sSC)
                    else:
                        on("tensor", fn)
                wait("vector", sSC)
                if qk == 1:
                    wait("vector", sKV, kv_dma_mark)
                if qk == 0:
                    on("vector", lambda v, cc=cc, tc=tc, pbi=pbi:
                       v.tensor_copy(q_rT[:, cc, 512 * tc:512 * (tc + 1)], p_qk[:, pbi, :]))
                else:
                    on("vector", lambda v, cc=cc, pbi=pbi:
                       v.tensor_copy(kcon[:, cc, :], p_qk[:, pbi, :]))
                bump("vector", sVE)
                evq[("qk", tc, qk * 4 + cc)] = sVE.n
        wait("vector", sVE)
        for cc in range(4):
            dstc = cc if tc == 0 else 4 + cc
            on("vector", lambda v, cc=cc, dstc=dstc: v.reduce_max(
                kmax_acc[:, dstc:dstc + 1], kcon[:, cc, :], AX.X, apply_absolute_value=True))
        bump("vector", sVE)
        if tc > 0:
            wait("vector", sVE)
            for cc in range(4):
                on("vector", lambda v, cc=cc: v.tensor_tensor(
                    kmax_acc[:, cc:cc + 1], kmax_acc[:, cc:cc + 1],
                    kmax_acc[:, 4 + cc:5 + cc], ALU.max))
            bump("vector", sVE)
        for tb4 in range(4):
            pbi = tb4 % 2
            wait("tensor", sVE, evv.get((tc, tb4 - 2), 0))
            for db in range(NDB):
                fn = lambda t, tb4=tb4, db=db, pbi=pbi: \
                    t.matmul(p_v[:, pbi, :], ln1T[:, db, 128 * tb4:128 * (tb4 + 1)],
                             wvr[:, db, :], start=(db == 0), stop=(db == NDB - 1))
                if db == NDB - 1:
                    inc("tensor", fn, sSC)
                else:
                    on("tensor", fn)
            wait("vector", sSC)
            wait("vector", sKV, kv_dma_mark)
            on("vector", lambda v, tb4=tb4, pbi=pbi: v.tensor_copy(vcon[:, tb4, :], p_v[:, pbi, :]))
            bump("vector", sVE)
            evv[(tc, tb4)] = sVE.n
        qkv_mark = sSC.n
        wait("sync", sVE)
        for cc in range(4):
            inc("sync", lambda s, cc=cc, tc=tc: s.dma_start(
                out=kc_dram.ap()[128 * cc:128 * (cc + 1), 512 * tc:512 * (tc + 1)],
                in_=kcon[:, cc, :]), sKV, 16)
        for tb4 in range(4):
            inc("sync", lambda s, tb4=tb4, tc=tc: s.dma_start(
                out=vc_dram.ap()[512 * tc + 128 * tb4:512 * tc + 128 * (tb4 + 1), :],
                in_=vcon[:, tb4, :]), sKV, 16)
        kv_dma_mark = sKV.n

    wait("sync", sVE)
    inc("sync", lambda s: s.dma_start(
        out=kmax_dram.ap().rearrange("(c p) -> p c", p=128), in_=kmax_acc[:, 0:4],
        allow_slow_non_contiguous=True), sKM, 16)
    wait("gpsimd", sKV)
    inc("gpsimd", lambda g: g.collective_compute(
        "AllGather", ALU.bypass, replica_groups=[[0, 1, 2, 3], [4, 5, 6, 7]],
        ins=[kc_dram.ap().opt()], outs=[kg_dram.ap().opt()]), sCC)
    inc("gpsimd", lambda g: g.collective_compute(
        "AllGather", ALU.bypass, replica_groups=[[0, 1, 2, 3], [4, 5, 6, 7]],
        ins=[vc_dram.ap().opt()], outs=[vg_dram.ap().opt()]), sCC)
    AGKV = sCC.n
    wait("gpsimd", sKM, 16)
    inc("gpsimd", lambda g: g.collective_compute(
        "AllGather", ALU.bypass, replica_groups=[[0, 1, 2, 3], [4, 5, 6, 7]],
        ins=[kmax_dram.ap().opt()], outs=[kmaxg_dram.ap().opt()]), sCC)
    KMX = sCC.n

    r.emit(nc, "phase_a")
    if limit == "a":
        return nc
    esa.close()
    esFA.close()

    esb = ExitStack()
    k_cc = sb(esb, "k_cc", [128, 2, 4, TOK], BF16)
    recrow = sb(esb, "recrow", [1, 1024], F32)
    v_full = sb(esb, "v_full", [128, NKT, 512], BF16)
    expT = sb(esb, "expT", [128, 4, 1024], BF16)

    kg_view = kg_dram.ap().rearrange("(r c p) t -> p c r t", p=128, c=4)
    vg_view = vg_dram.ap().rearrange("(n p) h -> p n h", p=128)
    wait("sync", sCC, AGKV - 1)
    inc("sync", lambda s: s.dma_start(out=k_cc[:, 0, :, :], in_=kg_view[:, 0, :, :]), sKC[0], 16)
    inc("sync", lambda s: s.dma_start(out=k_cc[:, 1, :, :], in_=kg_view[:, 1, :, :]), sKC[1], 16)
    wait("sync", sCC, AGKV)
    for part in range(4):
        inc("sync", lambda s, part=part: s.dma_start(
            out=v_full[:, 16 * part:16 * (part + 1), :],
            in_=vg_view[:, 16 * part:16 * (part + 1), :]), sKVL, 16)
    kv_ld = sKVL.n
    wait("sync", sCC, KMX)
    inc("sync", lambda s: s.dma_start(
        out=kmxt[:, :],
        in_=kmaxg_dram.ap().rearrange("(r c p) -> p (r c)", p=128, c=4),
        allow_slow_non_contiguous=True), sKM2, 16)

    esbias = ExitStack()
    p_b = [pst(esbias, "p_b%d" % i, [1, 4], F32) for i in range(4)]
    p_bb = pst(esbias, "p_bb", [128, 8], F32)

    wait("vector", sKM2, 16)
    for cc in range(4):
        on("vector", lambda v, cc=cc: v.reduce_max(
            nrm[:, cc:cc + 1], q_rT.ap()[:, cc, :], AX.X, apply_absolute_value=True))
    on("vector", lambda v: v.tensor_tensor(
        nrm[:, 4:8], kmxt[:, 0:4], kmxt[:, 4:8], ALU.max))
    bump("vector", sVE)
    wait("vector", sVE)
    on("vector", lambda v: v.tensor_tensor(
        nrm[:, 4:8], nrm[:, 4:8], kmxt[:, 8:12], ALU.max))
    bump("vector", sVE)
    wait("vector", sVE)
    on("vector", lambda v: v.tensor_tensor(
        nrm[:, 4:8], nrm[:, 4:8], kmxt[:, 12:16], ALU.max))
    bump("vector", sVE)
    wait("vector", sVE)
    for cc in range(4):
        on("vector", lambda v, cc=cc: v.tensor_tensor(
            nrm[:, 8 + cc:9 + cc], nrm[:, cc:cc + 1], nrm[:, 4 + cc:5 + cc], ALU.mult))
    bump("vector", sVE)
    wait("tensor", sVE)
    for s4 in range(4):
        for cc in range(4):
            fn = lambda t, s4=s4, cc=cc: \
                t.matmul(p_b[s4][0:1, cc:cc + 1], ones_colF[32 * s4:32 * (s4 + 1), :],
                         nrm[32 * s4:32 * (s4 + 1), 8 + cc:9 + cc],
                         start=True, stop=True, tile_position=(32 * s4, 0))
            if s4 == 3 and cc == 3:
                inc("tensor", fn, sTE)
            else:
                on("tensor", fn)
    wait("vector", sTE)
    for s4 in range(4):
        on("vector", lambda v, s4=s4: v.tensor_copy(brow[0:1, 4 * s4:4 * s4 + 4], p_b[s4][0:1, :]))
    bump("vector", sVE)
    wait("vector", sVE)
    for cc in range(4):
        for m in range(2):
            pp = 2 * cc + m
            on("vector", lambda v, cc=cc, m=m, pp=pp: v.tensor_tensor(
                brow[0:1, 16 + pp:17 + pp], brow[0:1, 4 * (2 * m) + cc:4 * (2 * m) + cc + 1],
                brow[0:1, 4 * (2 * m + 1) + cc:4 * (2 * m + 1) + cc + 1], ALU.max))
    bump("vector", sVE)
    wait("vector", sVE)
    inc("vector", lambda v: v.tensor_scalar(brow[0:1, 16:24], brow[0:1, 16:24], -1.0, None, ALU.mult), sVE)
    wait("tensor", sVE)
    inc("tensor", lambda t: t.matmul(p_bb[:, :], ones_r128[:, :], brow[0:1, 16:24],
                                     start=True, stop=True), sTE)
    wait("vector", sTE)
    inc("vector", lambda v: v.tensor_copy(biasp[:, :], p_bb[:, :]), sVE)
    bias_ready = sVE.n

    if debug:
        wait("sync", sVE, bias_ready)
        inc("sync", lambda sy: sy.dma_start(out=dbg_kg.ap()[:, :], in_=kg_dram.ap()[:, :]), sDBG, 16)
        inc("sync", lambda sy: sy.dma_start(out=dbg_vg.ap()[:, :], in_=vg_dram.ap()[:, :]), sDBG, 16)
        inc("sync", lambda sy: sy.dma_start(out=dbg_kc.ap()[:, :], in_=kc_dram.ap()[:, :]), sDBG, 16)
        inc("sync", lambda sy: sy.dma_start(out=dbg_vc.ap()[:, :], in_=vc_dram.ap()[:, :]), sDBG, 16)
        inc("sync", lambda sy: sy.dma_start(out=dbg_q.ap()[:, :, :], in_=q_rT[:, :, :]), sDBG, 16)
        inc("sync", lambda sy: sy.dma_start(out=dbg_fold.ap()[:, :, :], in_=fold[:, :, :]), sDBG, 16)
        inc("sync", lambda sy: sy.dma_start(out=dbg_bias.ap()[:, :], in_=biasp[:, :]), sDBG, 16)
    esbias.close()
    p_sc = pst(esb, "p_sc", [128, 2, 1024], F32)
    p_yA = pst(esb, "p_yA", [128, 512], F32)
    p_yB = pst(esb, "p_yB", [128, 512], F32)
    p_da = pst(esb, "p_da", [128, 512], F32)
    p_db = pst(esb, "p_db", [128, 512], F32)

    esq = ExitStack()
    q_bf = sb(esq, "q_bf", [128, 2048], BF16)
    q_aw = sb(esq, "q_aw", [128, 2048], BF16)
    q_sg = sb(esq, "q_sg", [128, 2048], BF16)
    q_ht = sb(esq, "q_ht", [128, 2048], BF16)
    q_gs = sb(esq, "q_gs", [128, 32], F32)
    q_sc = sb(esq, "q_sc", [128, 32], BF16)
    q_sf = sb(esq, "q_sf", [128, 32], F32)
    q_th = sb(esq, "q_th", [128, 32], F32)
    tstg = sb(esq, "tstg", [128, 2, 512], BF16)

    wgu_tiles = wgu_sl.ap().rearrange("(n p) d -> p n d", p=128)
    wpj_tiles = wpj_sl.ap().rearrange("(n p) h -> p n h", p=128)
    qn, pn = 8, 2
    tp = {"grp": 0}

    def quant_round(src_tiles, r0, nt, d, G, marks):
        wait("gpsimd", sVE, marks["aw_done"])
        wait("gpsimd", sSE, marks["sign"])
        inc("gpsimd", lambda g, s=src_tiles, r0=r0, nt=nt:
            g.dma_start(out=q_bf.ap().rearrange("p (n d) -> p n d", n=nt),
                        in_=s[:, r0:r0 + nt, :]), sGD, 16)
        ld = sGD.n
        wait("scalar", sGD, ld)
        wait("scalar", sVE, marks["ht_mult"])
        inc("scalar", lambda s: s.activation(out=q_sg[:, :], in_=q_bf[:, :], func=AF.Sign), sSE)
        inc("scalar", lambda s: s.activation(out=q_aw[:, :], in_=q_bf[:, :], func=AF.Abs), sSE)
        marks["sign"] = sSE.n
        wait("vector", sGD, ld)
        wait("vector", sTR[0], marks["tpA"])
        wait("vector", sTR[1], marks["tpB"])
        for t in range(nt):
            on("vector", lambda v, t=t, d=d, G=G:
               v.tensor_reduce(q_gs[:, G * t:G * (t + 1)],
                               q_bf.ap()[:, t * d:(t + 1) * d].rearrange("p (g k) -> p g k", g=G),
                               AX.X, ALU.add, apply_absolute_value=True))
        bump("vector", sVE)
        wait("vector", sVE)
        inc("vector", lambda v:
            v.tensor_scalar(q_sc[:, :], q_gs[:, :], 1.0 / 64, 1e-8, ALU.mult, ALU.max), sVE)
        marks["aw_done"] = sVE.n
        wait("vector", sVE)
        inc("vector", lambda v: v.tensor_copy(q_sf[:, :], q_sc[:, :]), sVE)
        wait("vector", sVE)
        inc("vector", lambda v:
            v.tensor_scalar(q_th[:, :], q_sf[:, :], 0.5009765625, None, ALU.mult), sVE)
        wait("vector", sVE)
        wait("vector", sSE, marks["sign"])
        for t in range(nt):
            for gg in range(G):
                c0 = t * d + 64 * gg
                sc0 = G * t + gg
                on("vector", lambda v, c0=c0, sc0=sc0:
                   v.tensor_scalar(q_ht[:, c0:c0 + 64], q_aw[:, c0:c0 + 64],
                                   q_th[:, sc0:sc0 + 1], q_sf[:, sc0:sc0 + 1],
                                   ALU.is_ge, ALU.mult))
        bump("vector", sVE)
        wait("vector", sVE)
        inc("vector", lambda v:
            v.tensor_tensor(q_ht[:, :], q_ht[:, :], q_sg[:, :], ALU.mult), sVE)
        marks["ht_mult"] = sVE.n

    def tpose_group(cols_js, dst_ap_fn, marks):
        pbi = tp["grp"] % 2
        wait("sync", sTS[pbi], 16 * (tp["grp"] // 2))
        wait("sync", sVE, marks["ht_mult"])
        for i4, c0 in enumerate(cols_js):
            inc("sync", lambda sy, c0=c0, i4=i4, pbi=pbi: sy.dma_start_transpose(
                tstg[:, pbi, 128 * i4:128 * (i4 + 1)], q_ht[:, c0:c0 + 128]), sTR[pbi], 16)
        wait("sync", sTR[pbi], sTR[pbi].n)
        inc("sync", lambda sy, pbi=pbi, nw=len(cols_js), f=dst_ap_fn:
            sy.dma_start(out=f(), in_=tstg[:, pbi, 0:128 * nw]), sTS[pbi], 16)
        tp["grp"] += 1

    qmarks = {"ht_mult": 0, "sign": 0, "aw_done": 0, "tpA": 0, "tpB": 0}
    quant_steps = []
    for rr in range((2 * HF // 128) // qn):
        def step_g(rr=rr):
            quant_round(wgu_tiles, qn * rr, qn, 256, 4, qmarks)
            for j in range(2):
                for hg in range(2):
                    cols_js = [(4 * hg + i4) * 256 + 128 * j for i4 in range(4)]
                    js2 = 2 * rr + hg
                    tpose_group(cols_js, lambda js2=js2, j=j: wguc_dram.ap()[js2, :, j, :], qmarks)
            qmarks["tpA"] = sTR[0].n
            qmarks["tpB"] = sTR[1].n
        quant_steps.append(step_g)
    for rr in range((D // 128) // pn):
        def step_p(rr=rr):
            quant_round(wpj_tiles, pn * rr, pn, 1024, 16, qmarks)
            for hj in range(8):
                cols_js = [t_i * 1024 + 128 * hj for t_i in range(2)]
                tpose_group(cols_js,
                            lambda hj=hj, rr=rr: wpjc_dram.ap()[128 * hj:128 * (hj + 1),
                                                                256 * rr:256 * (rr + 1)], qmarks)
            qmarks["tpA"] = sTR[0].n
            qmarks["tpB"] = sTR[1].n
        quant_steps.append(step_p)

    SCB = sSC.n

    wait("scalar", sVE, bias_ready)
    wait("tensor", sVE, bias_ready)
    wait("tensor", sKVL, kv_ld)
    EXB = sEX.n
    YMB = sYM.n
    yev = 0
    for cc in range(4):
        if cc < 2:
            wait("tensor", sKC[cc], 16)
        else:
            wait("tensor", sKC2[cc - 2], 16)
        for m in range(2):
            pp = 2 * cc + m
            sA, sB2 = 2 * m, 2 * m + 1
            hA, hB = 4 * cc + 2 * m, 4 * cc + 2 * m + 1
            for qc in range(4):
                combo = (cc * 2 + m) * 4 + qc
                if quant_steps:
                    quant_steps.pop(0)()
                if combo == 6:
                    wait("sync", sSC, SCB + 512)
                    inc("sync", lambda s: s.dma_start(
                        out=k_cc[:, 0, :, :], in_=kg_view[:, 2, :, :]), sKC2[0], 16)
                if combo == 14:
                    wait("sync", sSC, SCB + 1024)
                    inc("sync", lambda s: s.dma_start(
                        out=k_cc[:, 1, :, :], in_=kg_view[:, 3, :, :]), sKC2[1], 16)
                def y_group(kt):
                    st, sp = (kt == 0), (kt == NKT - 1)
                    gstep = combo * NKT + kt
                    pb = gstep % 4
                    wait("tensor", sEX, EXB + gstep + 1)
                    if kt == 0:
                        wait("tensor", sVE, yev)
                    on("tensor", lambda t, kt=kt, pb=pb, hA=hA, st=st, sp=sp:
                       t.matmul(p_yA[0:32, :], v_full[:, kt, 32 * hA:32 * (hA + 1)],
                                expT[:, pb, 0:512], start=st, stop=sp, tile_position=(0, 0)))
                    on("tensor", lambda t, kt=kt, pb=pb, hB=hB, st=st, sp=sp:
                       t.matmul(p_yB[32:64, :], v_full[:, kt, 32 * hB:32 * (hB + 1)],
                                expT[:, pb, 512:1024], start=st, stop=sp, tile_position=(0, 32)))
                    on("tensor", lambda t, pb=pb, st=st, sp=sp:
                       t.matmul(p_da[64:65, :], ones_col[:, :], expT[:, pb, 0:512],
                                start=st, stop=sp, tile_position=(0, 64)))
                    inc("tensor", lambda t, pb=pb, st=st, sp=sp:
                        t.matmul(p_db[96:97, :], ones_col[:, :], expT[:, pb, 512:1024],
                                 start=st, stop=sp, tile_position=(0, 96)), sYM)

                for kt in range(NKT):
                    gstep = combo * NKT + kt
                    pb = kt % 2
                    pbe = gstep % 4
                    rr_, jj_ = kt // 16, kt % 16
                    wait("tensor", sEX, EXB + gstep - 1)
                    if kt == 0:
                        wait("tensor", sVE, yev)
                    on("tensor", lambda t, sA=sA, cc=cc, rr_=rr_, jj_=jj_, qc=qc, pb=pb:
                       t.matmul(p_sc[:, pb, 0:512],
                                k_cc[32 * sA:32 * (sA + 1), cc % 2, rr_, 128 * jj_:128 * (jj_ + 1)],
                                q_rT[32 * sA:32 * (sA + 1), cc, 512 * qc:512 * (qc + 1)],
                                start=True, stop=True, tile_position=(32 * sA, 0)))
                    inc("tensor", lambda t, sB2=sB2, cc=cc, rr_=rr_, jj_=jj_, qc=qc, pb=pb:
                        t.matmul(p_sc[:, pb, 512:1024],
                                 k_cc[32 * sB2:32 * (sB2 + 1), cc % 2, rr_, 128 * jj_:128 * (jj_ + 1)],
                                 q_rT[32 * sB2:32 * (sB2 + 1), cc, 512 * qc:512 * (qc + 1)],
                                 start=True, stop=True, tile_position=(32 * sB2, 0)), sSC)
                    wait("scalar", sSC, SCB + gstep + 1)
                    wait("scalar", sYM, YMB + gstep - 3)
                    inc("scalar", lambda s, pb=pb, pbe=pbe, pp=pp: s.activation(
                        out=expT[:, pbe, :], in_=p_sc[:, pb, :], func=AF.Exp,
                        bias=biasp[:, pp:pp + 1]), sEX)
                    if kt >= 1:
                        y_group(kt - 1)
                y_group(NKT - 1)
                wait("vector", sYM)
                on("vector", lambda v: v.reciprocal(recrow[0:1, 0:512], p_da[64:65, :]))
                inc("vector", lambda v: v.reciprocal(recrow[0:1, 512:1024], p_db[96:97, :]), sVE)
                wait("tensor", sVE)
                on("tensor", lambda t: t.matmul(p_sc[0:32, 0, 0:512], ones_r32[:, :], recrow[0:1, 0:512],
                                                start=True, stop=True))
                inc("tensor", lambda t: t.matmul(p_sc[32:64, 0, 0:512], ones_r32[:, :], recrow[0:1, 512:1024],
                                                 start=True, stop=True), sTE)
                wait("vector", sTE)
                inc("vector", lambda v: v.tensor_copy(inv_sb[0:64, :], p_sc[0:64, 0, 0:512]), sVE)
                wait("vector", sVE)
                on("vector", lambda v, m=m, cc=cc, qc=qc: v.tensor_tensor(
                    y_rT[64 * m:64 * m + 32, cc, 512 * qc:512 * (qc + 1)],
                    p_yA[0:32, :], inv_sb[0:32, :], ALU.mult))
                inc("vector", lambda v, m=m, cc=cc, qc=qc: v.tensor_tensor(
                    y_rT[64 * m + 32:64 * (m + 1), cc, 512 * qc:512 * (qc + 1)],
                    p_yB[32:64, :], inv_sb[32:64, :], ALU.mult), sVE)
                yev = sVE.n

    wait("gpsimd", sTS[0])
    wait("gpsimd", sTS[1])
    inc("gpsimd", lambda g: g.collective_compute(
        "AllGather", ALU.bypass, replica_groups=[[0, 1, 2, 3, 4, 5, 6, 7]],
        ins=[wguc_dram.ap().opt()], outs=[wgug_dram.ap().opt()]), sCC)
    inc("gpsimd", lambda g: g.collective_compute(
        "AllGather", ALU.bypass, replica_groups=[[0, 1, 2, 3, 4, 5, 6, 7]],
        ins=[wpjc_dram.ap().opt()], outs=[wpjg_dram.ap().opt()]), sCC)
    AGW = sCC.n

    r.emit(nc, "phase_b")
    if limit == "b":
        return nc
    esq.close()
    esb.close()

    eso = ExitStack()
    lnscr = sb(eso, "lnscr_o", [128, D], F32)
    xo = sb(eso, "xo", [128, 2, D], F32)
    ao_sb = sb(eso, "ao_sb", [128, D], F32)
    x2_sb = sb(eso, "x2_sb", [128, 2, D], F32)
    ln3b = sb(eso, "ln3b", [128, 4, D], BF16)
    ln3tp = sb(eso, "ln3tp", [128, 2, 512], BF16)
    p_ao = pst(eso, "p_ao", [128, 2, 512], F32)
    p_l3 = pst(eso, "p_l3", [128, 2, 1024], BF16)

    wait("sync", sTS[0])
    wait("sync", sTS[1])
    res_mark = 0
    ln3ap_mark = 0
    l3tp_prev = 0
    l3ev = {}
    for tb in range(NTB):
        bi = tb % 2
        wait("sync", sVE, res_mark)
        inc("sync", lambda s, tb=tb, bi=bi: s.dma_start(out=xo[:, bi, :], in_=x_rows[:, tb, :]), sXL, 16)
        ldx = sXL.n
        for dc in range(4):
            pbi = dc % 2
            wait("tensor", sSE)
            for cc in range(4):
                fn = lambda t, tb=tb, cc=cc, dc=dc, pbi=pbi: \
                    t.matmul(p_ao[:, pbi, :], y_rT[:, cc, 128 * tb:128 * (tb + 1)],
                             fold[:, cc, 512 * dc:512 * (dc + 1)],
                             start=(cc == 0), stop=(cc == 3))
                if cc == 3:
                    inc("tensor", fn, sSC)
                else:
                    on("tensor", fn)
            wait("scalar", sSC)
            inc("scalar", lambda s, dc=dc, pbi=pbi:
                s.copy(ao_sb[:, 512 * dc:512 * (dc + 1)], p_ao[:, pbi, :]), sSE)
        ao_done = sSE.n
        wait("vector", sSE, ao_done)
        cols = ln_stats([lambda: ao_sb.ap()[:, :]], 0)
        wait("vector", sSE)
        nm, iv = cols[0]
        on("vector", lambda v, nm=nm, iv=iv: v.tensor_scalar(
            lnscr[:, :], ao_sb[:, :], stat[:, nm:nm + 1], stat[:, iv:iv + 1], ALU.add, ALU.mult))
        bump("vector", sVE)
        wait("vector", sVE)
        wait("vector", sXL, ldx)
        wait("vector", sX2S)
        inc("vector", lambda v, bi=bi: v.tensor_tensor(
            x2_sb[:, bi, :], lnscr[:, :], xo[:, bi, :], ALU.add), sVE)
        res_mark = sVE.n
        wait("sync", sVE)
        inc("sync", lambda s, tb=tb, bi=bi: s.dma_start(
            out=x2_dram.ap()[128 * tb:128 * (tb + 1), :], in_=x2_sb[:, bi, :]), sX2S, 16)
        cols = ln_stats([lambda bi=bi: x2_sb.ap()[:, bi, :]], 4)
        wait("vector", sSE)
        nm, iv = cols[0]
        wait("vector", sSC, l3tp_prev)
        inc("vector", lambda v, bi=bi, tb=tb, nm=nm, iv=iv: v.tensor_scalar(
            ln3b[:, tb % 4, :], x2_sb[:, bi, :], stat[:, nm:nm + 1], stat[:, iv:iv + 1],
            ALU.add, ALU.mult), sVE)
        ln3ap_mark = sVE.n
        if tb % 4 == 3:
            tg = tb // 4
            wait("tensor", sVE, ln3ap_mark)
            for db in range(NDB):
                pbi = db % 2
                if db >= 2:
                    wait("tensor", sVE, l3ev[(tg, db - 2)])
                for tb4 in range(4):
                    fn = lambda t, tb4=tb4, db=db, pbi=pbi: \
                        t.transpose(p_l3[:, pbi, 128 * tb4:128 * (tb4 + 1)],
                                    ln3b[:, tb4, 128 * db:128 * (db + 1)], identb[:, :])
                    if tb4 == 3:
                        inc("tensor", fn, sSC)
                    else:
                        on("tensor", fn)
                wait("vector", sSC)
                if db >= 2 or tg > 0:
                    wait("vector", sL3S)
                on("vector", lambda v, db=db, pbi=pbi: v.tensor_copy(ln3tp[:, pbi, :], p_l3[:, pbi, 0:512]))
                bump("vector", sVE)
                l3ev[(tg, db)] = sVE.n
                wait("sync", sVE, l3ev[(tg, db)])
                inc("sync", lambda s, db=db, tg=tg, pbi=pbi: s.dma_start(
                    out=ln3T_dram.ap()[db, :, 512 * tg:512 * (tg + 1)], in_=ln3tp[:, pbi, :]), sL3S, 16)
            l3tp_prev = sSC.n

    r.emit(nc, "phase_o")
    if limit == "o":
        return nc
    eso.close()
    es1.close()

    esg = ExitStack()
    lnscr = sb(esg, "lnscr_g", [128, D], F32)
    invrow = sb(esg, "invrow", [1, 512], F32)
    x2r = sb(esg, "x2r", [128, 2, D], F32)
    ln3T = sb(esg, "ln3T", [128, NDB, 512], BF16)
    hT = sb(esg, "hT", [128, 64, 512], BF16)
    wguf_g = sb(esg, "wguf_g", [128, 2, NDB, 512], BF16)
    wguf_u = sb(esg, "wguf_u", [128, 2, NDB, 512], BF16)
    wpjf = sb(esg, "wpjf", [128, 2, 1024], BF16)
    sgf = sb(esg, "sgf", [128, 512], F32)
    sqb = sb(esg, "sqb", [128, 2, 512], BF16)
    out_sb = sb(esg, "out_sb", [128, 2, D], BF16)
    ffn_sb = sb(esg, "ffn_sb", [128, 4, D], BF16)
    p_big = pst(esg, "p_big", [128, 8, 512], F32)

    wait("sync", sCC, AGW)
    wait("sync", sL3S)
    wait("sync", sX2S)
    x2_rows = x2_dram.ap().rearrange("(n p) d -> p n d", p=128)
    ln3T_view = ln3T_dram.ap().rearrange("db p t -> p db t")
    gu_cons = {}
    pj_cons = {}
    x2_mark = 0
    l3_mark = 0
    guend_mark = 0
    pjlast_mark = 0
    marks_h = {}
    marks_silu = {}
    marks_ss = {}
    marks_sq = {}
    pjev_mark = 0
    ln4_pending = []
    gmk = {"x2": 0}

    def ln4_step(job):
        sub_, tb4 = job
        n_t = 4 * sub_ + tb4
        bi = n_t % 2
        wait("sync", sVE, gmk["x2"])
        inc("sync", lambda s, n_t=n_t, bi=bi: s.dma_start(
            out=x2r[:, bi, :], in_=x2_rows[:, n_t, :]), sX2, 16)
        ldx2 = sX2.n
        cols = ln_stats([lambda tb4=tb4: ffn_sb.ap()[:, tb4, :]], 16)
        wait("vector", sSE)
        nm, iv = cols[0]
        on("vector", lambda v, tb4=tb4, nm=nm, iv=iv: v.tensor_scalar(
            lnscr[:, :], ffn_sb[:, tb4, :], stat[:, nm:nm + 1], stat[:, iv:iv + 1],
            ALU.add, ALU.mult))
        bump("vector", sVE)
        wait("vector", sVE)
        wait("vector", sX2, ldx2)
        if n_t >= 2:
            wait("vector", sOUT2[bi], 16 * ((n_t - 2) // 2 + 1))
        inc("vector", lambda v, bi=bi: v.tensor_tensor(
            out_sb[:, bi, :], lnscr[:, :], x2r[:, bi, :], ALU.add), sVE)
        gmk["x2"] = sVE.n
        wait("gpsimd", sVE)
        inc("gpsimd", lambda g, n_t=n_t, bi=bi: g.dma_start(
            out=y_out.ap()[128 * n_t:128 * (n_t + 1), :], in_=out_sb[:, bi, :]), sOUT2[bi], 16)

    for sub in range(NSUB):
        wait("sync", sSC, guend_mark)
        inc("sync", lambda s, sub=sub: s.dma_start(
            out=ln3T[:, :, :], in_=ln3T_view[:, :, 512 * sub:512 * (sub + 1)]), sL3, 16)
        l3_mark = sL3.n
        wait("tensor", sL3, l3_mark)
        wait("vector", sTE, pjlast_mark)
        wait("tensor", sVE, pjev_mark)
        for js in range(16):
            bi = js % 2
            gjs = 16 * sub + js
            wait("sync", sSC, gu_cons.get(gjs - 2, 0))
            inc("sync", lambda s, js=js, bi=bi: s.dma_start(
                out=wguf_g.ap()[:, bi, :, :].rearrange("p (r dbl) hl -> p r dbl hl", r=8),
                in_=wgug_dram.ap()[:, js, :, :, :].rearrange("r p dbl hl -> p r dbl hl")),
            sFWs[bi], 16)
            inc("sync", lambda s, js=js, bi=bi: s.dma_start(
                out=wguf_u.ap()[:, bi, :, :].rearrange("p (r dbl) hl -> p r dbl hl", r=8),
                in_=wgug_dram.ap()[:, 16 + js, :, :, :].rearrange("r p dbl hl -> p r dbl hl")),
            sFWs[bi], 16)
            ldw = sFWs[bi].n
            wait("tensor", sFWs[bi], ldw)
            for jj in range(4):
                jb = 4 * js + jj
                gjb = 64 * sub + jb
                wait("tensor", sSE, marks_silu.get(gjb - 2, 0))
                wait("tensor", sVE, marks_h.get(gjb - 2, 0))
                for db in range(NDB):
                    on("tensor", lambda t, bi=bi, db=db, jj=jj:
                       t.matmul(p_big[:, jj % 2, :], wguf_g[:, bi, db, 128 * jj:128 * (jj + 1)],
                                ln3T[:, db, :], start=(db == 0), stop=(db == NDB - 1)))
                for db in range(NDB):
                    fn = lambda t, bi=bi, db=db, jj=jj: \
                        t.matmul(p_big[:, 2 + jj % 2, :], wguf_u[:, bi, db, 128 * jj:128 * (jj + 1)],
                                 ln3T[:, db, :], start=(db == 0), stop=(db == NDB - 1))
                    if db == NDB - 1:
                        inc("tensor", fn, sSC)
                    else:
                        on("tensor", fn)
                gu_mm = sSC.n
                wait("scalar", sSC, gu_mm)
                wait("scalar", sVE, marks_h.get(gjb - 1, 0))
                inc("scalar", lambda s, jj=jj: s.activation(
                    out=sgf[:, :], in_=p_big[:, jj % 2, :], func=AF.Silu), sSE)
                marks_silu[gjb] = sSE.n
                wait("vector", sSE, marks_silu[gjb])
                inc("vector", lambda v, jb=jb, jj=jj: v.tensor_tensor(
                    hT[:, jb, :], sgf[:, :], p_big[:, 2 + jj % 2, :], ALU.mult), sVE)
                marks_h[gjb] = sVE.n
                wait("scalar", sVE, marks_h[gjb])
                wait("scalar", sQ2, marks_ss.get(gjb - 2, 0))
                inc("scalar", lambda s, jb=jb: s.activation(
                    out=sqb[:, jb % 2, :], in_=hT[:, jb, :], func=AF.Square), sSE)
                marks_sq[gjb] = sSE.n
                if jb >= 1:
                    wait("tensor", sSE, marks_sq[gjb - 1])
                    inc("tensor", lambda t, jb=jb: t.matmul(
                        p_big[0:1, 4, :], ones_col[:, :], sqb[:, (jb - 1) % 2, :],
                        start=(jb == 1), stop=False), sQ2)
                    marks_ss[gjb - 1] = sQ2.n
            gu_cons[gjs] = sSC.n
            if ln4_pending:
                ln4_step(ln4_pending.pop(0))
        wait("tensor", sSE, marks_sq[64 * sub + 63])
        inc("tensor", lambda t: t.matmul(
            p_big[0:1, 4, :], ones_col[:, :], sqb[:, 63 % 2, :],
            start=False, stop=True), sQ2)
        marks_ss[64 * sub + 63] = sQ2.n
        guend_mark = sSC.n
        wait("vector", sQ2)
        inc("vector", lambda v: v.tensor_scalar(
            invrow[0:1, :], p_big[0:1, 4, :], 1.0 / HF, RMS_EPS, ALU.mult, ALU.add), sVE)
        wait("scalar", sVE)
        inc("scalar", lambda s: s.activation(out=invrow[0:1, :], in_=invrow[0:1, :], func=AF.Ln), sSE)
        wait("scalar", sSE)
        inc("scalar", lambda s: s.activation(out=invrow[0:1, :], in_=invrow[0:1, :],
                                             func=AF.Exp, scale=-0.5), sSE)
        wait("tensor", sSE)
        for w4 in range(4):
            fn = lambda t, w4=w4: t.matmul(
                p_big[:, 5, w4:w4 + 1], invrow[0:1, 128 * w4:128 * (w4 + 1)], ones_colF[0:1, :],
                start=True, stop=True)
            if w4 == 3:
                inc("tensor", fn, sTE)
            else:
                on("tensor", fn)
        wait("vector", sTE)
        inc("vector", lambda v: v.tensor_copy(inv_t[:, :], p_big[:, 5, 0:4]), sVE)
        invt_done = sVE.n
        for pss in range(2):
            for hb in range(64):
                bi = hb % 2
                ghb = (2 * sub + pss) * 64 + hb
                wait("sync", sTE, pj_cons.get(ghb - 2, 0))
                inc("sync", lambda s, pss=pss, hb=hb, bi=bi: s.dma_start(
                    out=wpjf[:, bi, :],
                    in_=wpjg_dram.ap()[128 * hb:128 * (hb + 1), 1024 * pss:1024 * (pss + 1)]),
                sPJ[bi], 16)
                ldp = sPJ[bi].n
                wait("tensor", sPJ[bi], ldp)
                if hb == 0:
                    wait("tensor", sVE, invt_done if pss == 0 else sVE.n)
                for tb4 in range(4):
                    for dch in range(2):
                        st, sp = (hb == 0), (hb == 63)
                        fn = lambda t, tb4=tb4, dch=dch, hb=hb, bi=bi, st=st, sp=sp: \
                            t.matmul(p_big[:, 2 * tb4 + dch, :],
                                     hT[:, hb, 128 * tb4:128 * (tb4 + 1)],
                                     wpjf[:, bi, 512 * dch:512 * (dch + 1)],
                                     start=st, stop=sp)
                        if tb4 == 3 and dch == 1:
                            inc("tensor", fn, sTE)
                        else:
                            on("tensor", fn)
                pj_cons[ghb] = sTE.n
            pj_mm = sTE.n
            wait("vector", sTE, pj_mm)
            for tb4 in range(4):
                for dch in range(2):
                    dc = 2 * pss + dch
                    on("vector", lambda v, tb4=tb4, dch=dch, dc=dc: v.tensor_scalar(
                        ffn_sb[:, tb4, 512 * dc:512 * (dc + 1)], p_big[:, 2 * tb4 + dch, :],
                        inv_t[:, tb4:tb4 + 1], None, ALU.mult))
            bump("vector", sVE)
            pjev_mark = sVE.n
        pjlast_mark = sTE.n
        for tb4 in range(4):
            ln4_pending.append((sub, tb4))
        if sub == NSUB - 1:
            while ln4_pending:
                ln4_step(ln4_pending.pop(0))

    wait("gpsimd", sOUT2[0])
    wait("gpsimd", sOUT2[1])
    r.emit(nc, "phase_g")
    esg.close()
    es0.close()
    return nc


_NC_CACHE = {}


def kernel(**inputs):
    x = np.asarray(inputs["L0_attn_x"], dtype=np.float32)
    wq = np.ascontiguousarray(np.asarray(inputs["L0_attn_Wq"], dtype=np.float32))
    wk = np.ascontiguousarray(np.asarray(inputs["L0_attn_Wk"], dtype=np.float32))
    wv = np.ascontiguousarray(np.asarray(inputs["L0_attn_Wv"], dtype=np.float32))
    wo = np.ascontiguousarray(np.asarray(inputs["L0_attn_Wo"], dtype=np.float32))
    wdown = np.ascontiguousarray(np.asarray(inputs["L0_attn_Wdown"], dtype=np.float32))
    wup = np.ascontiguousarray(np.asarray(inputs["L0_attn_Wup"], dtype=np.float32))
    wgu = np.asarray(inputs["L0_ffn_w_gate_up"], dtype=np.float32)
    wpj = np.asarray(inputs["L0_ffn_w_proj"], dtype=np.float32)

    if "nc" not in _NC_CACHE:
        _NC_CACHE["nc"] = build_nc()
    nc = _NC_CACHE["nc"]

    in_maps = []
    for c in range(8):
        b, r0 = c // 4, (c % 4) * TOK
        in_maps.append({
            "x": np.ascontiguousarray(x[b, r0:r0 + TOK, :]),
            "Wq": wq, "Wk": wk, "Wv": wv, "Wo": wo,
            "Wdown": wdown, "Wup": wup,
            "wgu_sl": np.ascontiguousarray(wgu[:, 256 * c:256 * (c + 1)]),
            "wpj_sl": np.ascontiguousarray(wpj[:, 1024 * c:1024 * (c + 1)]),
        })
    res = run_bass_kernel_spmd(nc, in_maps, core_ids=list(range(8)))
    out = np.empty((2, T, D), dtype=np.float32)
    for c in range(8):
        b, r0 = c // 4, (c % 4) * TOK
        out[b, r0:r0 + TOK, :] = res.results[c]["y"]
    return out
